# version 26
# baseline (speedup 1.0000x reference)
import numpy as np
from contextlib import ExitStack

import concourse.bass as bass
import concourse.mybir as mybir
from concourse.bass_utils import run_bass_kernel_spmd

F32 = mybir.dt.float32
BF16 = mybir.dt.bfloat16
AF = mybir.ActivationFunctionType
ALU = mybir.AluOpType
AX = mybir.AxisListType

NCORE = 8
SEQ_PER_CORE = 4
SEQ_LEN = 2048
D = 1024
P = 128
NJ = SEQ_LEN // P
NT = SEQ_PER_CORE * NJ
NH = 16
HD = 64
LN_EPS = 1e-5
ATT_EPS = 1e-6
INV_D = 1.0 / D
SET = 8 * 16


def build_nc():
    nc = bass.Bass("TRN2", target_bir_lowering=False)

    x = nc.dram_tensor("x", [SEQ_PER_CORE, SEQ_LEN, D], F32, kind="ExternalInput").ap()
    wqT = nc.dram_tensor("wqT", [D, D], F32, kind="ExternalInput").ap()
    wkT = nc.dram_tensor("wkT", [D, D], F32, kind="ExternalInput").ap()
    wvT = nc.dram_tensor("wvT", [D, D], F32, kind="ExternalInput").ap()
    woT = nc.dram_tensor("woT", [D, D], F32, kind="ExternalInput").ap()
    wfT = nc.dram_tensor("wfT", [D, D], F32, kind="ExternalInput").ap()
    bq = nc.dram_tensor("bq", [1, D], F32, kind="ExternalInput").ap()
    bk = nc.dram_tensor("bk", [1, D], F32, kind="ExternalInput").ap()
    bv = nc.dram_tensor("bv", [1, D], F32, kind="ExternalInput").ap()
    bo = nc.dram_tensor("bo", [1, D], F32, kind="ExternalInput").ap()
    out = nc.dram_tensor("out", [SEQ_PER_CORE, SEQ_LEN, D], F32, kind="ExternalOutput").ap()
    y1scr = nc.dram_tensor("y1scr", [NT, P, D], F32, kind="Internal").ap()
    qscr = nc.dram_tensor("qscr", [NT, P, D], BF16, kind="Internal").ap()

    es = ExitStack()
    with es:
        sb = lambda name, shape, dt: es.enter_context(nc.sbuf_tensor(name, shape, dt))
        ps = lambda name, shape, dt: es.enter_context(nc.psum_tensor(name, shape, dt))
        sem = lambda name: es.enter_context(nc.semaphore(name))

        wq = sb("wq", [P, 8, D], BF16)
        wk = sb("wk", [P, 8, D], BF16)
        wv = sb("wv", [P, 8, D], BF16)
        wo = sb("wo", [P, 8, D], BF16)
        sw = sb("sw", [P, 8, D], BF16)
        stg = sb("stg", [P, 1, D], F32)
        bqr = sb("bqr", [1, D], BF16)
        bkr = sb("bkr", [1, D], BF16)
        bvr = sb("bvr", [1, D], BF16)
        bor = sb("bor", [1, D], BF16)
        ones_row = sb("ones_row", [1, P], BF16)
        ident = sb("ident", [P, P], BF16)
        kvz_sb = sb("kvz_sb", [P, SEQ_PER_CORE, 8, P], BF16)
        zexp = sb("zexp", [P, SEQ_PER_CORE, 8, 2], BF16)

        xt = sb("xt", [P, 4, D], F32)
        u0b = sb("u0b", [P, 2, D], BF16)
        u0fm = sb("u0fm", [P, 2, D], BF16)
        y1t = sb("y1t", [P, 2, D], F32)
        u1b = sb("u1b", [P, 2, D], BF16)
        u1fm = sb("u1fm", [P, 2, D], BF16)
        kbQ = sb("kbQ", [P, D], BF16)
        kbK = sb("kbK", [P, D], BF16)
        mnb = sb("mnb", [P, D], BF16)
        mxb = sb("mxb", [P, D], BF16)
        eb = sb("eb", [P, D], BF16)
        phiQ = sb("phiQ", [P, 2, D], BF16)
        phiK = sb("phiK", [P, D], BF16)
        qfm = sb("qfm", [P, 2, D], BF16)
        vb = sb("vb", [P, NH, 66], BF16)

        y1in = sb("y1in", [P, 2, D], F32)
        qfin = sb("qfin", [P, 2, D], BF16)
        attn = sb("attn", [P, 2, D], BF16)
        attnfm = sb("attnfm", [P, 2, D], BF16)
        y2t = sb("y2t", [P, 2, D], F32)
        u2b = sb("u2b", [P, 2, D], BF16)
        u2fm = sb("u2fm", [P, 2, D], BF16)
        y3t = sb("y3t", [P, 2, D], F32)
        ddb = sb("ddb", [P, NH], F32)
        rdb = sb("rdb", [P, NH], F32)

        red1b = sb("red1b", [P, 1], F32)
        sq1b = sb("sq1b", [P, 1], F32)
        red2b = sb("red2b", [P, 1], F32)
        sq2b = sb("sq2b", [P, 1], F32)
        red3b = sb("red3b", [P, 1], F32)
        sq3b = sb("sq3b", [P, 1], F32)
        mS1 = sb("mS1", [P, 3, 1], F32)
        mS2 = sb("mS2", [P, 3, 1], F32)
        mS3 = sb("mS3", [P, 3, 1], F32)
        negm1 = sb("negm1", [P, 4, 1], F32)
        negm2 = sb("negm2", [P, 2, 1], F32)
        negm3 = sb("negm3", [P, 2, 1], F32)
        lnb = sb("lnb", [P, 1], F32)
        vr1 = sb("vr1", [P, 4, 1], F32)
        vr2 = sb("vr2", [P, 2, 1], F32)
        vr3 = sb("vr3", [P, 2, 1], F32)
        r01 = sb("r01", [P, 1], F32)
        r02 = sb("r02", [P, 1], F32)
        r03 = sb("r03", [P, 1], F32)
        nb1 = sb("nb1", [P, 1], F32)
        nb2 = sb("nb2", [P, 1], F32)
        nb3 = sb("nb3", [P, 1], F32)

        tpA = ps("tpA", [P, 8, P], BF16)
        tpB = ps("tpB", [P, 8, P], BF16)
        pA1 = ps("pA1", [P, D], F32)
        pA2 = ps("pA2", [P, D], F32)
        pKa = ps("pKa", [P, 4, 66], F32)
        pKb = ps("pKb", [P, 4, 66], F32)

        pA1_pairs = pA1[:, :].rearrange("p (c n) -> p c n", n=P)
        pA1_heads = pA1[:, :].rearrange("p (g e) -> p g e", e=HD)
        pA2_heads = pA2[:, :].rearrange("p (g e) -> p g e", e=HD)
        pden = pKa[:, 0, 0:NH]

        gIota, gOnes, gVb, gKz, gBias = sem("gIota"), sem("gOnes"), sem("gVb"), sem("gKz"), sem("gBias")
        gWq, gWk, gWv, gWo = sem("gWq"), sem("gWk"), sem("gWv"), sem("gWo")
        gStg = sem("gStg")
        sSw = sem("sSw")
        dIn = [sem(f"dIn{i}") for i in range(4)]
        dY1o = [sem("dY1oA"), sem("dY1oB")]
        dQo = [sem("dQoA"), sem("dQoB")]
        dY1i = [sem("dY1iA"), sem("dY1iB")]
        dQi = [sem("dQiA"), sem("dQiB")]
        dY3o = [sem("dY3oA"), sem("dY3oB")]
        dQT = sem("dQT")
        vIdent = sem("vIdent")
        vVr1, vY1, vVr2 = sem("vVr1"), sem("vY1"), sem("vVr2")
        vKbQ, vPhiQ = sem("vKbQ"), sem("vPhiQ")
        vKbK, vPhiK = sem("vKbK"), sem("vPhiK")
        vAttn, vY2, vVr3, vY3 = sem("vAttn"), sem("vY2"), sem("vVr3"), sem("vY3")
        sR01, sU0, sU0fm, sR02, sU1, sU1fm = sem("sR01"), sem("sU0"), sem("sU0fm"), sem("sR02"), sem("sU1"), sem("sU1fm")
        sEQ, sEK, sVb, sKvzE = sem("sEQ"), sem("sEK"), sem("sVb"), sem("sKvzE")
        sAttnfm, sR03, sU2, sU2fm = sem("sAttnfm"), sem("sR03"), sem("sU2"), sem("sU2fm")
        sSq2, sSq3 = sem("sSq2"), sem("sSq3")
        vC = sem("vC")
        sC = sem("sC")
        tFfn1, tU0T, tU1T, tQ, tK, tV, tKvz = (sem("tFfn1"), sem("tU0T"), sem("tU1T"),
                                               sem("tQ"), sem("tK"), sem("tV"), sem("tKvz"))
        tNum, tAttnT, tWo, tU2T, tFfn2 = sem("tNum"), sem("tAttnT"), sem("tWo"), sem("tU2T"), sem("tFfn2")

        block = es.enter_context(nc.Block())

        @block.sync
        def _(e):
            for t0 in range(4):
                s, j = t0 // NJ, t0 % NJ
                e.dma_start(out=xt[:, t0 % 4, :], in_=x[s, j*P:(j+1)*P, :]).then_inc(dIn[t0 % 4], 16)
            for c in range(8):
                buf = stg[:, 0, :] if c % 2 == 0 else y3t[:, 0, :]
                if c >= 2:
                    e.wait_ge(sSw, c - 1)
                if c >= 1:
                    e.wait_ge(gStg, 16 * c)
                e.dma_start(out=buf, in_=wfT[c*P:(c+1)*P, :]).then_inc(gStg, 16)
            for t in range(NT):
                if t + 4 < NT:
                    tl = t + 4
                    s, j = tl // NJ, tl % NJ
                    e.wait_ge(sU0, t + 1)
                    e.wait_ge(vY1, t + 1)
                    e.wait_ge(dIn[tl % 4], 16 * (tl // 4))
                    e.dma_start(out=xt[:, tl % 4, :], in_=x[s, j*P:(j+1)*P, :]).then_inc(dIn[tl % 4], 16)
                e.wait_ge(vPhiQ, t + 1)
                if t >= 1:
                    e.wait_ge(dQT, SET * t)
                if t >= 2:
                    e.wait_ge(dQo[t % 2], 16 * (t // 2))
                for c in range(8):
                    e.dma_start(out=qfm[:, t % 2, c*P:(c+1)*P], in_=phiQ[:, t % 2, c*P:(c+1)*P],
                                transpose=True).then_inc(dQT, 16)
            for t in range(NT):
                s, j = t // NJ, t % NJ
                e.wait_ge(vY3, 2 * (t + 1))
                if t >= 2:
                    e.wait_ge(dY3o[t % 2], 16 * (t // 2))
                e.dma_start(out=out[s, j*P:(j+1)*P, :], in_=y3t[:, t % 2, :]).then_inc(dY3o[t % 2], 16)
            e.wait_ge(dY3o[0], 16 * (NT // 2))
            e.wait_ge(dY3o[1], 16 * (NT // 2))

        @block.gpsimd
        def _(e):
            e.iota(attn[:, 0, 0:P], pattern=[[1, P]], channel_multiplier=-1,
                   allow_small_or_imprecise_dtypes=True).then_inc(gIota, 1)
            e.memset(ones_row[:, :], 1.0).then_inc(gOnes, 1)
            e.memset(vb[:, :, 64:65], 1.0).then_inc(gVb, 1)
            e.memset(vb[:, :, 65:66], 0.0).then_inc(gVb, 1)
            e.memset(kvz_sb[:, :, :, :], 0.0).then_inc(gKz, 1)
            e.memset(zexp[:, :, :, :], 0.0).then_inc(gKz, 1)
            for i, (bias_dst, bias_src) in enumerate([(bqr, bq), (bkr, bk), (bvr, bv), (bor, bo)]):
                if i >= 1:
                    e.wait_ge(gBias, 16 * i)
                e.dma_start(out=bias_dst[:, :], in_=bias_src).then_inc(gBias, 16)
            e.dma_start(out=wq[:, :, :], in_=wqT.rearrange("(c p) d -> p c d", p=P)).then_inc(gWq, 16)
            e.dma_start(out=wk[:, :, :], in_=wkT.rearrange("(c p) d -> p c d", p=P)).then_inc(gWk, 16)
            e.dma_start(out=wv[:, :, :], in_=wvT.rearrange("(c p) d -> p c d", p=P)).then_inc(gWv, 16)
            e.dma_start(out=wo[:, :, :], in_=woT.rearrange("(c p) d -> p c d", p=P)).then_inc(gWo, 16)
            for t in range(NT):
                e.wait_ge(vY1, t + 1)
                if t >= 2:
                    e.wait_ge(dY1o[t % 2], 16 * (t // 2))
                e.dma_start(out=y1scr[t], in_=y1t[:, t % 2, :]).then_inc(dY1o[t % 2], 16)
                e.wait_ge(dQT, SET * (t + 1))
                if t >= 2:
                    e.wait_ge(dQo[t % 2], 16 * (t // 2))
                e.dma_start(out=qscr[t], in_=qfm[:, t % 2, :]).then_inc(dQo[t % 2], 16)
                if t == min(34, NT - 1):
                    for tb in (0, 1):
                        e.dma_start(out=qfin[:, tb, :], in_=qscr[tb]).then_inc(dQi[tb], 16)
                        e.dma_start(out=y1in[:, tb, :], in_=y1scr[tb]).then_inc(dY1i[tb], 16)
            for s_ in (dY1o[0], dY1o[1], dQo[0], dQo[1]):
                e.wait_ge(s_, 16 * (NT // 2))
            for t in range(2, NT):
                e.wait_ge(tNum, t - 1)
                e.wait_ge(dQi[t % 2], 16 * (t // 2))
                e.dma_start(out=qfin[:, t % 2, :], in_=qscr[t]).then_inc(dQi[t % 2], 16)
                e.wait_ge(vY2, t - 1)
                e.wait_ge(dY1i[t % 2], 16 * (t // 2))
                e.dma_start(out=y1in[:, t % 2, :], in_=y1scr[t]).then_inc(dY1i[t % 2], 16)

        def ln_stats_multi(v, vsync, specs):
            for (redb, sqb, vrX, pi, negm, mS) in specs:
                ins = v.tensor_scalar(mS[:, 0, :], redb[:, :], INV_D, None, ALU.mult)
            vsync(v, ins)
            for (redb, sqb, vrX, pi, negm, mS) in specs:
                v.tensor_scalar(mS[:, 1, :], sqb[:, :], INV_D, LN_EPS, ALU.mult, ALU.add)
                v.tensor_tensor(mS[:, 2, :], mS[:, 0, :], mS[:, 0, :], ALU.mult)
                ins = v.tensor_scalar(negm[:, pi, :], mS[:, 0, :], -1.0, None, ALU.mult)
            vsync(v, ins)
            out = []
            for (redb, sqb, vrX, pi, negm, mS) in specs:
                out.append(v.tensor_tensor(vrX[:, pi, :], mS[:, 1, :], mS[:, 2, :], ALU.subtract))
            return out

        vc_n = [0]
        def vsync(v, ins):
            vc_n[0] += 1
            ins.then_inc(vC, 1)
            v.wait_ge(vC, vc_n[0])

        @block.vector
        def _(v):
            v.wait_ge(gIota, 1)
            v.tensor_single_scalar(ident[:, :], attn[:, 0, 0:P], 0.0, ALU.is_equal).then_inc(vIdent, 1)
            for t0 in (0, 1, 2):
                v.wait_ge(dIn[t0], 16)
                v.reduce_sum(red1b[:, :], xt[:, t0, :], axis=AX.X)
                vsync(v, v.scalar_tensor_tensor(y2t[:, 0, :], xt[:, t0, :], 0.0, xt[:, t0, :],
                                       ALU.bypass, ALU.mult, accum_out=sq1b[:, :]))
                ln_stats_multi(v, vsync, [(red1b, sq1b, vr1, t0, negm1, mS1)])[0].then_inc(vVr1, 1)
            for t0 in (0, 1):
                v.wait_ge(tU0T, t0 + 1)
                v.tensor_copy(u0fm[:, t0 % 2, 0:512], tpB[:, 0:4, :].rearrange("p a b -> p (a b)")).then_inc(sU0fm, 1)
                v.tensor_copy(u0fm[:, t0 % 2, 512:1024], tpB[:, 4:8, :].rearrange("p a b -> p (a b)")).then_inc(sU0fm, 1)
            v.wait_ge(tFfn1, 1)
            v.scalar_tensor_tensor(y1t[:, 0, :], pA1[:, :], 0.5, xt[:, 0, :],
                                   ALU.mult, ALU.add, accum_out=red2b[:, :]).then_inc(vY1, 1)
            v.wait_ge(sSq2, 1)
            ln_stats_multi(v, vsync, [(red2b, sq2b, vr2, 0, negm2, mS2)])[0].then_inc(vVr2, 1)
            for t in range(NT):
                p2 = t % 2
                v.wait_ge(tU1T, t + 1)
                v.tensor_copy(u1fm[:, p2, 0:512], tpA[:, 0:4, :].rearrange("p a b -> p (a b)")).then_inc(sU1fm, 1)
                v.tensor_copy(u1fm[:, p2, 512:1024], tpA[:, 4:8, :].rearrange("p a b -> p (a b)")).then_inc(sU1fm, 1)
                if t + 1 < NT:
                    v.wait_ge(tFfn1, t + 2)
                    v.wait_ge(sU1, t + 1)
                    if t >= 1:
                        v.wait_ge(dY1o[(t + 1) % 2], 16 * ((t + 1) // 2))
                    v.scalar_tensor_tensor(y1t[:, (t + 1) % 2, :], pA1[:, :], 0.5, xt[:, (t + 1) % 4, :],
                                           ALU.mult, ALU.add, accum_out=red2b[:, :]).then_inc(vY1, 1)
                if t + 3 < NT:
                    tf = t + 3
                    v.wait_ge(dIn[tf % 4], 16 * (tf // 4 + 1))
                    v.reduce_sum(red1b[:, :], xt[:, tf % 4, :], axis=AX.X)
                    vsync(v, v.scalar_tensor_tensor(y2t[:, 0, :], xt[:, tf % 4, :], 0.0, xt[:, tf % 4, :],
                                           ALU.bypass, ALU.mult, accum_out=sq1b[:, :]))
                specs = []
                if t + 1 < NT:
                    v.wait_ge(sSq2, t + 2)
                    if t >= 2:
                        v.wait_ge(sR02, t)
                    specs.append((red2b, sq2b, vr2, (t + 1) % 2, negm2, mS2))
                if t + 3 < NT:
                    if t >= 1:
                        v.wait_ge(sR01, t)
                    specs.append((red1b, sq1b, vr1, (t + 3) % 4, negm1, mS1))
                if specs:
                    outs = ln_stats_multi(v, vsync, specs)
                    if t + 1 < NT:
                        outs[0].then_inc(vVr2, 1)
                    if t + 3 < NT:
                        outs[-1].then_inc(vVr1, 1)
                if t + 2 < NT:
                    v.wait_ge(tU0T, t + 3)
                    v.tensor_copy(u0fm[:, t % 2, 0:512], tpB[:, 0:4, :].rearrange("p a b -> p (a b)")).then_inc(sU0fm, 1)
                    v.tensor_copy(u0fm[:, t % 2, 512:1024], tpB[:, 4:8, :].rearrange("p a b -> p (a b)")).then_inc(sU0fm, 1)
                if t >= 2:
                    v.wait_ge(dQT, SET * (t - 1))
                v.wait_ge(sEQ, t + 1)
                v.tensor_tensor(phiQ[:, p2, :], eb[:, :], mxb[:, :], ALU.add).then_inc(vPhiQ, 1)
                v.wait_ge(sEK, t + 1)
                v.tensor_tensor(phiK[:, :], eb[:, :], mxb[:, :], ALU.add).then_inc(vPhiK, 1)
            for t in range(NT + 3):
                if 3 <= t <= NT + 2:
                    tf = t - 3
                    v.wait_ge(tFfn2, t - 2)
                    if tf >= 2:
                        v.wait_ge(dY3o[tf % 2], 16 * (tf // 2))
                    v.scalar_tensor_tensor(y3t[:, tf % 2, 0:512], pA2[:, 0:512], 0.5,
                                           y2t[:, tf % 2, 0:512],
                                           ALU.mult, ALU.add).then_inc(vY3, 1)
                    v.scalar_tensor_tensor(y3t[:, tf % 2, 512:1024], pA2[:, 512:1024], 0.5,
                                           y2t[:, tf % 2, 512:1024],
                                           ALU.mult, ALU.add).then_inc(vY3, 1)
                if t < NT:
                    v.wait_ge(tNum, t + 1)
                    vsync(v, v.tensor_scalar(ddb[:, :], pden, ATT_EPS, None, ALU.add))
                    vsync(v, v.reciprocal(rdb[:, :], ddb[:, :]))
                    if t >= 2:
                        v.wait_ge(tAttnT, t - 1)
                    v.tensor_tensor(attn[:, t % 2, :].rearrange("p (g e) -> p g e", e=HD),
                                    pA1_heads,
                                    rdb[:, :].unsqueeze(2).broadcast_to([P, NH, HD]),
                                    ALU.mult).then_inc(vAttn, 1)
                if 1 <= t <= NT:
                    tw = t - 1
                    if 3 <= t <= NT + 2:
                        v.wait_ge(vY3, 2 * (t - 2))
                    v.wait_ge(tWo, t)
                    v.wait_ge(dY1i[tw % 2], 16 * (tw // 2 + 1))
                    if t >= 3:
                        v.wait_ge(sU2, t - 2)
                    v.scalar_tensor_tensor(y2t[:, tw % 2, :], pA2[:, :], 0.0, y1in[:, tw % 2, :],
                                           ALU.bypass, ALU.add, accum_out=red3b[:, :]).then_inc(vY2, 1)
                    v.wait_ge(sSq3, t)
                    if t >= 3:
                        v.wait_ge(sR03, t - 2)
                    ln_stats_multi(v, vsync, [(red3b, sq3b, vr3, tw % 2, negm3, mS3)])[0].then_inc(vVr3, 1)

        sc_n = [0]
        def ssync(e, ins):
            sc_n[0] += 1
            ins.then_inc(sC, 1)
            e.wait_ge(sC, sc_n[0])

        def ln_apply(e, ssync, vrX, pi, negm, r0, nb, incR, nR):
            ssync(e, e.activation(lnb[:, :], vrX[:, pi, :], AF.Ln))
            ssync(e, e.activation(r0[:, :], lnb[:, :], AF.Exp, scale=-0.5))
            e.activation(nb[:, :], negm[:, pi, :], AF.Identity, scale=r0[:, :]).then_inc(incR, 1)
            e.wait_ge(incR, nR)

        @block.scalar
        def _(e):
            for c in range(8):
                e.wait_ge(gStg, 16 * (c + 1))
                e.sign(sw[:, c, :], stg[:, 0, :] if c % 2 == 0 else y3t[:, 0, :]).then_inc(sSw, 1)
            for t0 in (0, 1, 2):
                e.wait_ge(vVr1, t0 + 1)
                ln_apply(e, ssync, vr1, t0, negm1, r01, nb1, sR01, t0 + 1)
                if t0 == 2:
                    e.wait_ge(tU0T, 1)
                e.activation(u0b[:, t0 % 2, :], xt[:, t0, :], AF.Identity,
                             scale=r01[:, :], bias=nb1[:, :]).then_inc(sU0, 1)
            e.wait_ge(vY1, 1)
            e.activation(u2b[:, 0, :], y1t[:, 0, :], AF.Square,
                         accum_out=sq2b[:, :]).then_inc(sSq2, 1)
            e.wait_ge(vVr2, 1)
            ln_apply(e, ssync, vr2, 0, negm2, r02, nb2, sR02, 1)
            e.activation(u1b[:, 0, :], y1t[:, 0, :], AF.Identity,
                         scale=r02[:, :], bias=nb2[:, :]).then_inc(sU1, 1)
            for t in range(NT):
                if t + 1 < NT:
                    e.wait_ge(vY1, t + 2)
                    e.activation(u2b[:, 0, :], y1t[:, (t + 1) % 2, :], AF.Square,
                                 accum_out=sq2b[:, :]).then_inc(sSq2, 1)
                e.wait_ge(tQ, t + 1)
                e.copy(kbQ[:, 0:512], pA2[:, 0:512]).then_inc(vKbQ, 1)
                e.copy(kbQ[:, 512:1024], pA2[:, 512:1024]).then_inc(vKbQ, 1)
                e.wait_ge(vKbQ, 2 * t + 2)
                if t + 1 < NT:
                    e.wait_ge(vVr2, t + 2)
                    ln_apply(e, ssync, vr2, (t + 1) % 2, negm2, r02, nb2, sR02, t + 2)
                    if t >= 1:
                        e.wait_ge(tU1T, t)
                    e.activation(u1b[:, (t + 1) % 2, :], y1t[:, (t + 1) % 2, :], AF.Identity,
                                 scale=r02[:, :], bias=nb2[:, :]).then_inc(sU1, 1)
                if t >= 1:
                    e.wait_ge(vPhiK, t)
                ssync(e, e.activation(mnb[:, :], kbQ[:, :], AF.Relu, scale=-1.0))
                e.activation(eb[:, :], mnb[:, :], AF.Exp, scale=-1.0)
                e.wait_ge(tK, t + 1)
                e.copy(kbK[:, 0:512], pA2[:, 0:512]).then_inc(vKbK, 1)
                e.copy(kbK[:, 512:1024], pA2[:, 512:1024]).then_inc(vKbK, 1)
                e.wait_ge(vKbK, 2 * t + 2)
                e.activation(mxb[:, :], kbQ[:, :], AF.Relu).then_inc(sEQ, 1)
                e.wait_ge(vPhiQ, t + 1)
                ssync(e, e.activation(mnb[:, :], kbK[:, :], AF.Relu, scale=-1.0))
                e.activation(eb[:, :], mnb[:, :], AF.Exp, scale=-1.0)
                e.activation(mxb[:, :], kbK[:, :], AF.Relu).then_inc(sEK, 1)
                if t + 3 < NT:
                    tf = t + 3
                    e.wait_ge(vVr1, tf + 1)
                    ln_apply(e, ssync, vr1, tf % 4, negm1, r01, nb1, sR01, tf + 1)
                    e.wait_ge(tU0T, t + 2)
                    e.activation(u0b[:, tf % 2, :], xt[:, tf % 4, :], AF.Identity,
                                 scale=r01[:, :], bias=nb1[:, :]).then_inc(sU0, 1)
                e.wait_ge(tV, t + 1)
                e.copy(vb[:, :, 0:64], pA2_heads).then_inc(sVb, 1)
                if t % NJ == NJ - 1:
                    s = t // NJ
                    e.wait_ge(tKvz, t + 1)
                    for bi, pK in [(0, pKa), (1, pKb)]:
                        e.copy(kvz_sb[0:64, s, 4*bi:4*bi+4, 0:64], pK[0:64, :, 0:64])
                        e.copy(kvz_sb[64:128, s, 4*bi:4*bi+4, 64:128], pK[64:128, :, 0:64])
                        e.copy(zexp[0:64, s, 4*bi:4*bi+4, 0:1], pK[0:64, :, 64:65])
                        mm = e.copy(zexp[64:128, s, 4*bi:4*bi+4, 1:2], pK[64:128, :, 64:65])
                    mm.then_inc(sKvzE, 1)
            for t in range(1, NT + 2):
                if t <= NT:
                    e.wait_ge(tAttnT, t)
                    e.copy(attnfm[:, (t - 1) % 2, 0:512],
                           tpA[:, 0:4, :].rearrange("p a b -> p (a b)")).then_inc(sAttnfm, 1)
                    e.copy(attnfm[:, (t - 1) % 2, 512:1024],
                           tpA[:, 4:8, :].rearrange("p a b -> p (a b)")).then_inc(sAttnfm, 1)
                if 2 <= t:
                    e.wait_ge(tU2T, t - 1)
                    e.copy(u2fm[:, (t - 2) % 2, 0:512],
                           tpB[:, 0:4, :].rearrange("p a b -> p (a b)")).then_inc(sU2fm, 1)
                    e.copy(u2fm[:, (t - 2) % 2, 512:1024],
                           tpB[:, 4:8, :].rearrange("p a b -> p (a b)")).then_inc(sU2fm, 1)
                if t <= NT:
                    tw = t - 1
                    e.wait_ge(vY2, t)
                    e.activation(u1b[:, 0, :], y2t[:, tw % 2, :], AF.Square,
                                 accum_out=sq3b[:, :]).then_inc(sSq3, 1)
                    e.wait_ge(vVr3, t)
                    ln_apply(e, ssync, vr3, tw % 2, negm3, r03, nb3, sR03, t)
                    if t >= 3:
                        e.wait_ge(tU2T, t - 2)
                    e.activation(u2b[:, tw % 2, :], y2t[:, tw % 2, :], AF.Identity,
                                 scale=r03[:, :], bias=nb3[:, :]).then_inc(sU2, 1)

        def transposes8(e, src, tp, incsem):
            for c in range(8):
                mm = e.transpose(tp[:, c, :], src[:, c*P:(c+1)*P], ident[:, :])
            mm.then_inc(incsem, 1)

        def proj(e, lhs_fm, w_sb, dst, bias_row, incsem, cw=None, hw=None):
            last = (bias_row is None)
            if hw is None:
                for c in range(8):
                    if cw is not None and c % 4 == 0:
                        e.wait_ge(cw[0], cw[1] + c // 4)
                    for h in range(2):
                        mm = e.matmul(dst[:, h*512:(h+1)*512], lhs_fm[:, c*P:(c+1)*P],
                                      w_sb[:, c, h*512:(h+1)*512], start=(c == 0),
                                      stop=(last and c == 7))
                if bias_row is not None:
                    for h in range(2):
                        mm = e.matmul(dst[:, h*512:(h+1)*512], ones_row[0:1, :],
                                      bias_row[0:1, h*512:(h+1)*512], start=False, stop=True)
            else:
                for h in range(2):
                    e.wait_ge(hw[0], hw[1] + h)
                    for c in range(8):
                        mm = e.matmul(dst[:, h*512:(h+1)*512], lhs_fm[:, c*P:(c+1)*P],
                                      w_sb[:, c, h*512:(h+1)*512], start=(c == 0),
                                      stop=(last and c == 7))
                    if bias_row is not None:
                        mm = e.matmul(dst[:, h*512:(h+1)*512], ones_row[0:1, :],
                                      bias_row[0:1, h*512:(h+1)*512], start=False, stop=True)
            mm.then_inc(incsem, 1)

        @block.tensor
        def _(e):
            e.wait_ge(vIdent, 1)
            e.wait_ge(sSw, 8)
            e.wait_ge(gOnes, 1)
            e.wait_ge(gBias, 16 * 4)
            e.wait_ge(gVb, 2)
            e.wait_ge(gKz, 2)
            e.wait_ge(sU0, 1)
            transposes8(e, u0b[:, 0, :], tpB, tU0T)
            e.wait_ge(sU0, 2)
            e.wait_ge(sU0fm, 2)
            transposes8(e, u0b[:, 1, :], tpB, tU0T)
            proj(e, u0fm[:, 0, :], sw, pA1[:, :], None, tFfn1)
            e.wait_ge(sU1, 1)
            transposes8(e, u1b[:, 0, :], tpA, tU1T)
            for t in range(NT):
                s, j = t // NJ, t % NJ
                if t + 1 < NT:
                    e.wait_ge(vY1, t + 1)
                    proj(e, u0fm[:, (t + 1) % 2, :], sw, pA1[:, :], None, tFfn1,
                         cw=(sU0fm, 2 * t + 3))
                if t >= 1:
                    tl = t - 1
                    sl, jl = tl // NJ, tl % NJ
                    e.wait_ge(vPhiK, tl + 1)
                    e.wait_ge(sVb, tl + 1)
                    if jl == 0:
                        if sl >= 1:
                            e.wait_ge(sKvzE, sl)
                        for pK in (pKa, pKb):
                            for c4 in range(4):
                                e.matmul(pK[:, c4, 0:66], ones_row[0:1, :],
                                         ident[0:1, 1:67], start=(c4 == 0),
                                         stop=False, skip_group_check=True)
                    for h in range(NH):
                        pK = pKa if h < 8 else pKb
                        c4 = (h // 2) % 4
                        r = HD * (h % 2)
                        mm = e.matmul(pK[r:r+HD, c4, 0:66], phiK[:, h*HD:(h+1)*HD],
                                      vb[:, h, 0:66],
                                      start=False,
                                      stop=(jl == NJ - 1 and h in (7, NH - 1)),
                                      tile_position=(0, r), skip_group_check=True)
                    mm.then_inc(tKvz, 1)
                if t == 0:
                    e.wait_ge(gWq, 16)
                if t >= 1:
                    e.wait_ge(sVb, t)
                proj(e, u1fm[:, t % 2, :], wq, pA2[:, :], bqr, tQ,
                     cw=(sU1fm, 2 * t + 1))
                if t + 2 < NT:
                    e.wait_ge(sU0, t + 3)
                    e.wait_ge(sU0fm, 2 * t + 4)
                    transposes8(e, u0b[:, t % 2, :], tpB, tU0T)
                if t == 0:
                    e.wait_ge(gWk, 16)
                proj(e, u1fm[:, t % 2, :], wk, pA2[:, :], bkr, tK,
                     hw=(vKbQ, 2 * t + 1))
                if t + 1 < NT:
                    e.wait_ge(sU1, t + 2)
                    e.wait_ge(sU1fm, 2 * t + 2)
                    transposes8(e, u1b[:, (t + 1) % 2, :], tpA, tU1T)
                if t == 0:
                    e.wait_ge(gWv, 16)
                proj(e, u1fm[:, t % 2, :], wv, pA2[:, :], bvr, tV,
                     hw=(vKbK, 2 * t + 1))
            e.wait_ge(vPhiK, NT)
            e.wait_ge(sVb, NT)
            for h in range(NH):
                pK = pKa if h < 8 else pKb
                c4 = (h // 2) % 4
                r = HD * (h % 2)
                mm = e.matmul(pK[r:r+HD, c4, 0:66], phiK[:, h*HD:(h+1)*HD], vb[:, h, 0:66],
                              start=False, stop=(h in (7, NH - 1)), tile_position=(0, r),
                              skip_group_check=True)
            mm.then_inc(tKvz, 1)
            e.wait_ge(gWo, 16)
            e.wait_ge(sKvzE, SEQ_PER_CORE)
            e.wait_ge(vKbQ, 2 * NT)
            e.wait_ge(sVb, NT)
            for t in range(NT + 3):
                if 3 <= t <= NT + 2:
                    e.wait_ge(vY2, min(t - 1, NT))
                    if t > NT:
                        e.wait_ge(vY3, 2 * (t - 3))
                    proj(e, u2fm[:, (t - 3) % 2, :], sw, pA2[:, :], None, tFfn2,
                         cw=(sU2fm, 2 * t - 5))
                if t < NT:
                    s = t // NJ
                    p2 = t % 2
                    e.wait_ge(dQi[p2], 16 * (t // 2 + 1))
                    if t >= 1:
                        e.wait_ge(vAttn, t)
                    for c in range(8):
                        e.matmul(pA1_pairs[:, c, :], qfin[:, p2, c*P:(c+1)*P], kvz_sb[:, s, c, :],
                                 start=True, stop=True)
                    for c in range(8):
                        mm = e.matmul(pden[:, 2*c:2*c+2], qfin[:, p2, c*P:(c+1)*P], zexp[:, s, c, :],
                                      start=True, stop=True)
                    mm.then_inc(tNum, 1)
                if 2 <= t <= NT + 1:
                    e.wait_ge(sU2, t - 1)
                    if t >= 3:
                        e.wait_ge(sU2fm, 2 * t - 4)
                    transposes8(e, u2b[:, (t - 2) % 2, :], tpB, tU2T)
                if 1 <= t <= NT:
                    e.wait_ge(sAttnfm, 2 * t)
                    proj(e, attnfm[:, (t - 1) % 2, :], wo, pA2[:, :], bor, tWo,
                         hw=(vY3, 2 * t - 5) if t >= 3 else None)
                if t < NT:
                    e.wait_ge(vAttn, t + 1)
                    if t >= 1:
                        e.wait_ge(sAttnfm, 2 * t)
                    transposes8(e, attn[:, t % 2, :], tpA, tAttnT)

    return nc


def kernel(L0_attn_x, L0_attn_Wq, L0_attn_bq, L0_attn_Wk, L0_attn_bk,
           L0_attn_Wv, L0_attn_bv, L0_attn_Wo, L0_attn_bo, L0_ffn_weight):
    xs = np.ascontiguousarray(np.asarray(L0_attn_x, dtype=np.float32))
    wqT = np.ascontiguousarray(np.asarray(L0_attn_Wq, dtype=np.float32).T)
    wkT = np.ascontiguousarray(np.asarray(L0_attn_Wk, dtype=np.float32).T)
    wvT = np.ascontiguousarray(np.asarray(L0_attn_Wv, dtype=np.float32).T)
    woT = np.ascontiguousarray(np.asarray(L0_attn_Wo, dtype=np.float32).T)
    wfT = np.ascontiguousarray(np.asarray(L0_ffn_weight, dtype=np.float32).T)
    biases = {
        "bq": np.ascontiguousarray(np.asarray(L0_attn_bq, dtype=np.float32).reshape(1, D)),
        "bk": np.ascontiguousarray(np.asarray(L0_attn_bk, dtype=np.float32).reshape(1, D)),
        "bv": np.ascontiguousarray(np.asarray(L0_attn_bv, dtype=np.float32).reshape(1, D)),
        "bo": np.ascontiguousarray(np.asarray(L0_attn_bo, dtype=np.float32).reshape(1, D)),
    }
    nc = build_nc()
    in_maps = []
    for k in range(NCORE):
        m = {"x": np.ascontiguousarray(xs[k*SEQ_PER_CORE:(k+1)*SEQ_PER_CORE]),
             "wqT": wqT, "wkT": wkT, "wvT": wvT, "woT": woT, "wfT": wfT}
        m.update(biases)
        in_maps.append(m)
    res = run_bass_kernel_spmd(nc, in_maps, core_ids=list(range(NCORE)))
    return np.concatenate([np.asarray(res.results[k]["out"]) for k in range(NCORE)], axis=0)


# version 27
# speedup vs baseline: 1.0286x; 1.0286x over previous
import numpy as np
from contextlib import ExitStack

import concourse.bass as bass
import concourse.mybir as mybir
from concourse.bass_utils import run_bass_kernel_spmd

F32 = mybir.dt.float32
BF16 = mybir.dt.bfloat16
AF = mybir.ActivationFunctionType
ALU = mybir.AluOpType
AX = mybir.AxisListType

NCORE = 8
SEQ_PER_CORE = 4
SEQ_LEN = 2048
D = 1024
P = 128
NJ = SEQ_LEN // P
NT = SEQ_PER_CORE * NJ
NH = 16
HD = 64
LN_EPS = 1e-5
ATT_EPS = 1e-6
INV_D = 1.0 / D
SET = 8 * 16


def build_nc():
    nc = bass.Bass("TRN2", target_bir_lowering=False)

    x = nc.dram_tensor("x", [SEQ_PER_CORE, SEQ_LEN, D], F32, kind="ExternalInput").ap()
    wqT = nc.dram_tensor("wqT", [D, D], F32, kind="ExternalInput").ap()
    wkT = nc.dram_tensor("wkT", [D, D], F32, kind="ExternalInput").ap()
    wvT = nc.dram_tensor("wvT", [D, D], F32, kind="ExternalInput").ap()
    woT = nc.dram_tensor("woT", [D, D], F32, kind="ExternalInput").ap()
    wfT = nc.dram_tensor("wfT", [D, D], F32, kind="ExternalInput").ap()
    bq = nc.dram_tensor("bq", [1, D], F32, kind="ExternalInput").ap()
    bk = nc.dram_tensor("bk", [1, D], F32, kind="ExternalInput").ap()
    bv = nc.dram_tensor("bv", [1, D], F32, kind="ExternalInput").ap()
    bo = nc.dram_tensor("bo", [1, D], F32, kind="ExternalInput").ap()
    out = nc.dram_tensor("out", [SEQ_PER_CORE, SEQ_LEN, D], F32, kind="ExternalOutput").ap()
    y1scr = nc.dram_tensor("y1scr", [NT, P, D], F32, kind="Internal").ap()
    qscr = nc.dram_tensor("qscr", [NT, P, D], BF16, kind="Internal").ap()

    es = ExitStack()
    with es:
        sb = lambda name, shape, dt: es.enter_context(nc.sbuf_tensor(name, shape, dt))
        ps = lambda name, shape, dt: es.enter_context(nc.psum_tensor(name, shape, dt))
        sem = lambda name: es.enter_context(nc.semaphore(name))

        wq = sb("wq", [P, 8, D], BF16)
        wk = sb("wk", [P, 8, D], BF16)
        wv = sb("wv", [P, 8, D], BF16)
        wo = sb("wo", [P, 8, D], BF16)
        sw = sb("sw", [P, 8, D], BF16)
        stg = sb("stg", [P, 1, D], F32)
        bqr = sb("bqr", [1, D], BF16)
        bkr = sb("bkr", [1, D], BF16)
        bvr = sb("bvr", [1, D], BF16)
        bor = sb("bor", [1, D], BF16)
        ones_row = sb("ones_row", [1, P], BF16)
        ident = sb("ident", [P, P], BF16)
        kvz_sb = sb("kvz_sb", [P, SEQ_PER_CORE, 8, P], BF16)
        zexp = sb("zexp", [P, SEQ_PER_CORE, 8, 2], BF16)

        xt = sb("xt", [P, 4, D], F32)
        u0b = sb("u0b", [P, 2, D], BF16)
        u0fm = sb("u0fm", [P, 2, D], BF16)
        y1t = sb("y1t", [P, 2, D], F32)
        u1b = sb("u1b", [P, 2, D], BF16)
        u1fm = sb("u1fm", [P, 2, D], BF16)
        kbQ = sb("kbQ", [P, D], BF16)
        kbK = sb("kbK", [P, D], BF16)
        mnb = sb("mnb", [P, D], BF16)
        mxb = sb("mxb", [P, D], BF16)
        eb = sb("eb", [P, D], BF16)
        phiQ = sb("phiQ", [P, 2, D], BF16)
        phiK = sb("phiK", [P, D], BF16)
        qfm = sb("qfm", [P, 2, D], BF16)
        vb = sb("vb", [P, NH, 66], BF16)

        y1in = sb("y1in", [P, 2, D], F32)
        qfin = sb("qfin", [P, 2, D], BF16)
        attn = sb("attn", [P, 2, D], BF16)
        attnfm = sb("attnfm", [P, 2, D], BF16)
        y2t = sb("y2t", [P, 2, D], F32)
        u2b = sb("u2b", [P, 2, D], BF16)
        u2fm = sb("u2fm", [P, 2, D], BF16)
        y3t = sb("y3t", [P, 2, D], F32)
        ddb = sb("ddb", [P, NH], F32)
        rdb = sb("rdb", [P, NH], F32)

        red1b = sb("red1b", [P, 1], F32)
        sq1b = sb("sq1b", [P, 1], F32)
        red2b = sb("red2b", [P, 1], F32)
        sq2b = sb("sq2b", [P, 1], F32)
        red3b = sb("red3b", [P, 1], F32)
        sq3b = sb("sq3b", [P, 1], F32)
        mS1 = sb("mS1", [P, 3, 1], F32)
        mS2 = sb("mS2", [P, 3, 1], F32)
        mS3 = sb("mS3", [P, 3, 1], F32)
        negm1 = sb("negm1", [P, 4, 1], F32)
        negm2 = sb("negm2", [P, 2, 1], F32)
        negm3 = sb("negm3", [P, 2, 1], F32)
        lnb = sb("lnb", [P, 1], F32)
        vr1 = sb("vr1", [P, 4, 1], F32)
        vr2 = sb("vr2", [P, 2, 1], F32)
        vr3 = sb("vr3", [P, 2, 1], F32)
        r01 = sb("r01", [P, 1], F32)
        r02 = sb("r02", [P, 1], F32)
        r03 = sb("r03", [P, 1], F32)
        nb1 = sb("nb1", [P, 1], F32)
        nb2 = sb("nb2", [P, 1], F32)
        nb3 = sb("nb3", [P, 1], F32)

        tpA = ps("tpA", [P, 8, P], BF16)
        tpB = ps("tpB", [P, 8, P], BF16)
        pA1 = ps("pA1", [P, D], F32)
        pA2 = ps("pA2", [P, D], F32)
        pKa = ps("pKa", [P, 4, 66], F32)
        pKb = ps("pKb", [P, 4, 66], F32)

        pA1_pairs = pA1[:, :].rearrange("p (c n) -> p c n", n=P)
        pA1_heads = pA1[:, :].rearrange("p (g e) -> p g e", e=HD)
        pA2_heads = pA2[:, :].rearrange("p (g e) -> p g e", e=HD)
        pden = pKa[:, 0, 0:NH]

        gIota, gOnes, gVb, gKz, gBias = sem("gIota"), sem("gOnes"), sem("gVb"), sem("gKz"), sem("gBias")
        gWq, gWk, gWv, gWo = sem("gWq"), sem("gWk"), sem("gWv"), sem("gWo")
        gStg = sem("gStg")
        sSw = sem("sSw")
        dIn = [sem(f"dIn{i}") for i in range(4)]
        dY1o = [sem("dY1oA"), sem("dY1oB")]
        dQo = [sem("dQoA"), sem("dQoB")]
        dY1i = [sem("dY1iA"), sem("dY1iB")]
        dQi = [sem("dQiA"), sem("dQiB")]
        dY3o = [sem("dY3oA"), sem("dY3oB")]
        dQT = sem("dQT")
        vIdent = sem("vIdent")
        vVr1, vY1, vVr2 = sem("vVr1"), sem("vY1"), sem("vVr2")
        vKbQ, vPhiQ = sem("vKbQ"), sem("vPhiQ")
        vKbK, vPhiK = sem("vKbK"), sem("vPhiK")
        vAttn, vY2, vVr3, vY3 = sem("vAttn"), sem("vY2"), sem("vVr3"), sem("vY3")
        sR01, sU0, sU0fm, sR02, sU1, sU1fm = sem("sR01"), sem("sU0"), sem("sU0fm"), sem("sR02"), sem("sU1"), sem("sU1fm")
        sEQ, sEK, sVb, sKvzE = sem("sEQ"), sem("sEK"), sem("sVb"), sem("sKvzE")
        sAttnfm, sR03, sU2, sU2fm = sem("sAttnfm"), sem("sR03"), sem("sU2"), sem("sU2fm")
        sSq2, sSq3 = sem("sSq2"), sem("sSq3")
        vC = sem("vC")
        sC = sem("sC")
        tFfn1, tU0T, tU1T, tQ, tK, tV, tKvz = (sem("tFfn1"), sem("tU0T"), sem("tU1T"),
                                               sem("tQ"), sem("tK"), sem("tV"), sem("tKvz"))
        tNum, tAttnT, tWo, tU2T, tFfn2 = sem("tNum"), sem("tAttnT"), sem("tWo"), sem("tU2T"), sem("tFfn2")

        block = es.enter_context(nc.Block())

        @block.sync
        def _(e):
            for t0 in range(4):
                s, j = t0 // NJ, t0 % NJ
                e.dma_start(out=xt[:, t0 % 4, :], in_=x[s, j*P:(j+1)*P, :]).then_inc(dIn[t0 % 4], 16)
            for c in range(8):
                buf = stg[:, 0, :] if c % 2 == 0 else y3t[:, 0, :]
                if c >= 2:
                    e.wait_ge(sSw, c - 1)
                if c >= 1:
                    e.wait_ge(gStg, 16 * c)
                e.dma_start(out=buf, in_=wfT[c*P:(c+1)*P, :]).then_inc(gStg, 16)
            for t in range(NT):
                if t + 4 < NT:
                    tl = t + 4
                    s, j = tl // NJ, tl % NJ
                    e.wait_ge(sU0, t + 1)
                    e.wait_ge(vY1, t + 1)
                    e.wait_ge(dIn[tl % 4], 16 * (tl // 4))
                    e.dma_start(out=xt[:, tl % 4, :], in_=x[s, j*P:(j+1)*P, :]).then_inc(dIn[tl % 4], 16)
                e.wait_ge(vPhiQ, t + 1)
                if t >= 1:
                    e.wait_ge(dQT, SET * t)
                if t >= 2:
                    e.wait_ge(dQo[t % 2], 16 * (t // 2))
                for c in range(8):
                    e.dma_start(out=qfm[:, t % 2, c*P:(c+1)*P], in_=phiQ[:, t % 2, c*P:(c+1)*P],
                                transpose=True).then_inc(dQT, 16)
            for t in range(NT):
                s, j = t // NJ, t % NJ
                e.wait_ge(vY3, t + 1)
                if t >= 2:
                    e.wait_ge(dY3o[t % 2], 16 * (t // 2))
                e.dma_start(out=out[s, j*P:(j+1)*P, :], in_=y3t[:, t % 2, :]).then_inc(dY3o[t % 2], 16)
            e.wait_ge(dY3o[0], 16 * (NT // 2))
            e.wait_ge(dY3o[1], 16 * (NT // 2))

        @block.gpsimd
        def _(e):
            e.iota(attn[:, 0, 0:P], pattern=[[1, P]], channel_multiplier=-1,
                   allow_small_or_imprecise_dtypes=True).then_inc(gIota, 1)
            e.memset(ones_row[:, :], 1.0).then_inc(gOnes, 1)
            e.memset(vb[:, :, 64:65], 1.0).then_inc(gVb, 1)
            e.memset(vb[:, :, 65:66], 0.0).then_inc(gVb, 1)
            e.memset(kvz_sb[:, :, :, :], 0.0).then_inc(gKz, 1)
            e.memset(zexp[:, :, :, :], 0.0).then_inc(gKz, 1)
            for i, (bias_dst, bias_src) in enumerate([(bqr, bq), (bkr, bk), (bvr, bv), (bor, bo)]):
                if i >= 1:
                    e.wait_ge(gBias, 16 * i)
                e.dma_start(out=bias_dst[:, :], in_=bias_src).then_inc(gBias, 16)
            e.dma_start(out=wq[:, :, :], in_=wqT.rearrange("(c p) d -> p c d", p=P)).then_inc(gWq, 16)
            e.dma_start(out=wk[:, :, :], in_=wkT.rearrange("(c p) d -> p c d", p=P)).then_inc(gWk, 16)
            e.dma_start(out=wv[:, :, :], in_=wvT.rearrange("(c p) d -> p c d", p=P)).then_inc(gWv, 16)
            e.dma_start(out=wo[:, :, :], in_=woT.rearrange("(c p) d -> p c d", p=P)).then_inc(gWo, 16)
            for t in range(NT):
                e.wait_ge(vY1, t + 1)
                if t >= 2:
                    e.wait_ge(dY1o[t % 2], 16 * (t // 2))
                e.dma_start(out=y1scr[t], in_=y1t[:, t % 2, :]).then_inc(dY1o[t % 2], 16)
                e.wait_ge(dQT, SET * (t + 1))
                if t >= 2:
                    e.wait_ge(dQo[t % 2], 16 * (t // 2))
                e.dma_start(out=qscr[t], in_=qfm[:, t % 2, :]).then_inc(dQo[t % 2], 16)
                if t == min(34, NT - 1):
                    for tb in (0, 1):
                        e.dma_start(out=qfin[:, tb, :], in_=qscr[tb]).then_inc(dQi[tb], 16)
                        e.dma_start(out=y1in[:, tb, :], in_=y1scr[tb]).then_inc(dY1i[tb], 16)
            for s_ in (dY1o[0], dY1o[1], dQo[0], dQo[1]):
                e.wait_ge(s_, 16 * (NT // 2))
            for t in range(2, NT):
                e.wait_ge(tNum, t - 1)
                e.wait_ge(dQi[t % 2], 16 * (t // 2))
                e.dma_start(out=qfin[:, t % 2, :], in_=qscr[t]).then_inc(dQi[t % 2], 16)
                e.wait_ge(vY2, t - 1)
                e.wait_ge(dY1i[t % 2], 16 * (t // 2))
                e.dma_start(out=y1in[:, t % 2, :], in_=y1scr[t]).then_inc(dY1i[t % 2], 16)

        def ln_stats_multi(v, vsync, specs):
            for (redb, sqb, vrX, pi, negm, mS) in specs:
                ins = v.tensor_scalar(mS[:, 0, :], redb[:, :], INV_D, None, ALU.mult)
            vsync(v, ins)
            for (redb, sqb, vrX, pi, negm, mS) in specs:
                v.tensor_scalar(mS[:, 1, :], sqb[:, :], INV_D, LN_EPS, ALU.mult, ALU.add)
                v.tensor_tensor(mS[:, 2, :], mS[:, 0, :], mS[:, 0, :], ALU.mult)
                ins = v.tensor_scalar(negm[:, pi, :], mS[:, 0, :], -1.0, None, ALU.mult)
            vsync(v, ins)
            out = []
            for (redb, sqb, vrX, pi, negm, mS) in specs:
                out.append(v.tensor_tensor(vrX[:, pi, :], mS[:, 1, :], mS[:, 2, :], ALU.subtract))
            return out

        vc_n = [0]
        def vsync(v, ins):
            vc_n[0] += 1
            ins.then_inc(vC, 1)
            v.wait_ge(vC, vc_n[0])

        @block.vector
        def _(v):
            v.wait_ge(gIota, 1)
            v.tensor_single_scalar(ident[:, :], attn[:, 0, 0:P], 0.0, ALU.is_equal).then_inc(vIdent, 1)
            for t0 in (0, 1, 2):
                v.wait_ge(dIn[t0], 16)
                v.reduce_sum(red1b[:, :], xt[:, t0, :], axis=AX.X)
                vsync(v, v.scalar_tensor_tensor(y2t[:, 0, :], xt[:, t0, :], 0.0, xt[:, t0, :],
                                       ALU.bypass, ALU.mult, accum_out=sq1b[:, :]))
                ln_stats_multi(v, vsync, [(red1b, sq1b, vr1, t0, negm1, mS1)])[0].then_inc(vVr1, 1)
            for t0 in (0, 1):
                v.wait_ge(tU0T, t0 + 1)
                v.tensor_copy(u0fm[:, t0 % 2, 0:512], tpB[:, 0:4, :].rearrange("p a b -> p (a b)")).then_inc(sU0fm, 1)
                v.tensor_copy(u0fm[:, t0 % 2, 512:1024], tpB[:, 4:8, :].rearrange("p a b -> p (a b)")).then_inc(sU0fm, 1)
            v.wait_ge(tFfn1, 1)
            v.scalar_tensor_tensor(y1t[:, 0, :], pA1[:, :], 0.5, xt[:, 0, :],
                                   ALU.mult, ALU.add, accum_out=red2b[:, :]).then_inc(vY1, 1)
            v.wait_ge(sSq2, 1)
            ln_stats_multi(v, vsync, [(red2b, sq2b, vr2, 0, negm2, mS2)])[0].then_inc(vVr2, 1)
            for t in range(NT):
                p2 = t % 2
                v.wait_ge(tU1T, t + 1)
                v.tensor_copy(u1fm[:, p2, 0:512], tpA[:, 0:4, :].rearrange("p a b -> p (a b)")).then_inc(sU1fm, 1)
                v.tensor_copy(u1fm[:, p2, 512:1024], tpA[:, 4:8, :].rearrange("p a b -> p (a b)")).then_inc(sU1fm, 1)
                if t + 1 < NT:
                    v.wait_ge(tFfn1, t + 2)
                    v.wait_ge(sU1, t + 1)
                    if t >= 1:
                        v.wait_ge(dY1o[(t + 1) % 2], 16 * ((t + 1) // 2))
                    v.scalar_tensor_tensor(y1t[:, (t + 1) % 2, :], pA1[:, :], 0.5, xt[:, (t + 1) % 4, :],
                                           ALU.mult, ALU.add, accum_out=red2b[:, :]).then_inc(vY1, 1)
                if t + 3 < NT:
                    tf = t + 3
                    v.wait_ge(dIn[tf % 4], 16 * (tf // 4 + 1))
                    v.reduce_sum(red1b[:, :], xt[:, tf % 4, :], axis=AX.X)
                    vsync(v, v.scalar_tensor_tensor(y2t[:, 0, :], xt[:, tf % 4, :], 0.0, xt[:, tf % 4, :],
                                           ALU.bypass, ALU.mult, accum_out=sq1b[:, :]))
                specs = []
                if t + 1 < NT:
                    v.wait_ge(sSq2, t + 2)
                    if t >= 2:
                        v.wait_ge(sR02, t)
                    specs.append((red2b, sq2b, vr2, (t + 1) % 2, negm2, mS2))
                if t + 3 < NT:
                    if t >= 1:
                        v.wait_ge(sR01, t)
                    specs.append((red1b, sq1b, vr1, (t + 3) % 4, negm1, mS1))
                if specs:
                    outs = ln_stats_multi(v, vsync, specs)
                    if t + 1 < NT:
                        outs[0].then_inc(vVr2, 1)
                    if t + 3 < NT:
                        outs[-1].then_inc(vVr1, 1)
                if t + 2 < NT:
                    v.wait_ge(tU0T, t + 3)
                    v.tensor_copy(u0fm[:, t % 2, 0:512], tpB[:, 0:4, :].rearrange("p a b -> p (a b)")).then_inc(sU0fm, 1)
                    v.tensor_copy(u0fm[:, t % 2, 512:1024], tpB[:, 4:8, :].rearrange("p a b -> p (a b)")).then_inc(sU0fm, 1)
                if t >= 2:
                    v.wait_ge(dQT, SET * (t - 1))
                v.wait_ge(sEQ, t + 1)
                v.tensor_tensor(phiQ[:, p2, :], eb[:, :], mxb[:, :], ALU.add).then_inc(vPhiQ, 1)
                v.wait_ge(sEK, t + 1)
                v.tensor_tensor(phiK[:, :], eb[:, :], mxb[:, :], ALU.add).then_inc(vPhiK, 1)
            for t in range(NT + 3):
                if 3 <= t <= NT + 2:
                    tf = t - 3
                    v.wait_ge(tFfn2, t - 2)
                    if tf >= 2:
                        v.wait_ge(dY3o[tf % 2], 16 * (tf // 2))
                    v.scalar_tensor_tensor(y3t[:, tf % 2, :], pA2[:, :], 0.5, y2t[:, tf % 2, :],
                                           ALU.mult, ALU.add).then_inc(vY3, 1)
                if t < NT:
                    v.wait_ge(tNum, t + 1)
                    vsync(v, v.tensor_scalar(ddb[:, :], pden, ATT_EPS, None, ALU.add))
                    vsync(v, v.reciprocal(rdb[:, :], ddb[:, :]))
                    if t >= 2:
                        v.wait_ge(tAttnT, t - 1)
                    v.tensor_tensor(attn[:, t % 2, :].rearrange("p (g e) -> p g e", e=HD),
                                    pA1_heads,
                                    rdb[:, :].unsqueeze(2).broadcast_to([P, NH, HD]),
                                    ALU.mult).then_inc(vAttn, 1)
                if 1 <= t <= NT:
                    tw = t - 1
                    if 3 <= t <= NT + 2:
                        v.wait_ge(vY3, t - 2)
                    v.wait_ge(tWo, t)
                    v.wait_ge(dY1i[tw % 2], 16 * (tw // 2 + 1))
                    if t >= 3:
                        v.wait_ge(sU2, t - 2)
                    v.scalar_tensor_tensor(y2t[:, tw % 2, :], pA2[:, :], 0.0, y1in[:, tw % 2, :],
                                           ALU.bypass, ALU.add, accum_out=red3b[:, :]).then_inc(vY2, 1)
                    v.wait_ge(sSq3, t)
                    if t >= 3:
                        v.wait_ge(sR03, t - 2)
                    ln_stats_multi(v, vsync, [(red3b, sq3b, vr3, tw % 2, negm3, mS3)])[0].then_inc(vVr3, 1)

        sc_n = [0]
        def ssync(e, ins):
            sc_n[0] += 1
            ins.then_inc(sC, 1)
            e.wait_ge(sC, sc_n[0])

        def ln_apply(e, ssync, vrX, pi, negm, r0, nb, incR, nR):
            ssync(e, e.activation(lnb[:, :], vrX[:, pi, :], AF.Ln))
            ssync(e, e.activation(r0[:, :], lnb[:, :], AF.Exp, scale=-0.5))
            e.activation(nb[:, :], negm[:, pi, :], AF.Identity, scale=r0[:, :]).then_inc(incR, 1)
            e.wait_ge(incR, nR)

        @block.scalar
        def _(e):
            for c in range(8):
                e.wait_ge(gStg, 16 * (c + 1))
                e.sign(sw[:, c, :], stg[:, 0, :] if c % 2 == 0 else y3t[:, 0, :]).then_inc(sSw, 1)
            for t0 in (0, 1, 2):
                e.wait_ge(vVr1, t0 + 1)
                ln_apply(e, ssync, vr1, t0, negm1, r01, nb1, sR01, t0 + 1)
                if t0 == 2:
                    e.wait_ge(tU0T, 1)
                e.activation(u0b[:, t0 % 2, :], xt[:, t0, :], AF.Identity,
                             scale=r01[:, :], bias=nb1[:, :]).then_inc(sU0, 1)
            e.wait_ge(vY1, 1)
            e.activation(u2b[:, 0, :], y1t[:, 0, :], AF.Square,
                         accum_out=sq2b[:, :]).then_inc(sSq2, 1)
            e.wait_ge(vVr2, 1)
            ln_apply(e, ssync, vr2, 0, negm2, r02, nb2, sR02, 1)
            e.activation(u1b[:, 0, :], y1t[:, 0, :], AF.Identity,
                         scale=r02[:, :], bias=nb2[:, :]).then_inc(sU1, 1)
            for t in range(NT):
                if t + 1 < NT:
                    e.wait_ge(vY1, t + 2)
                    e.activation(u2b[:, 0, :], y1t[:, (t + 1) % 2, :], AF.Square,
                                 accum_out=sq2b[:, :]).then_inc(sSq2, 1)
                e.wait_ge(tQ, t + 1)
                e.copy(kbQ[:, 0:512], pA2[:, 0:512]).then_inc(vKbQ, 1)
                e.copy(kbQ[:, 512:1024], pA2[:, 512:1024]).then_inc(vKbQ, 1)
                e.wait_ge(vKbQ, 2 * t + 2)
                if t + 1 < NT:
                    e.wait_ge(vVr2, t + 2)
                    ln_apply(e, ssync, vr2, (t + 1) % 2, negm2, r02, nb2, sR02, t + 2)
                    if t >= 1:
                        e.wait_ge(tU1T, t)
                    e.activation(u1b[:, (t + 1) % 2, :], y1t[:, (t + 1) % 2, :], AF.Identity,
                                 scale=r02[:, :], bias=nb2[:, :]).then_inc(sU1, 1)
                if t >= 1:
                    e.wait_ge(vPhiK, t)
                ssync(e, e.activation(mnb[:, :], kbQ[:, :], AF.Relu, scale=-1.0))
                e.activation(eb[:, :], mnb[:, :], AF.Exp, scale=-1.0)
                e.wait_ge(tK, t + 1)
                e.copy(kbK[:, 0:512], pA2[:, 0:512]).then_inc(vKbK, 1)
                e.copy(kbK[:, 512:1024], pA2[:, 512:1024]).then_inc(vKbK, 1)
                e.wait_ge(vKbK, 2 * t + 2)
                e.activation(mxb[:, :], kbQ[:, :], AF.Relu).then_inc(sEQ, 1)
                e.wait_ge(vPhiQ, t + 1)
                ssync(e, e.activation(mnb[:, :], kbK[:, :], AF.Relu, scale=-1.0))
                e.activation(eb[:, :], mnb[:, :], AF.Exp, scale=-1.0)
                e.activation(mxb[:, :], kbK[:, :], AF.Relu).then_inc(sEK, 1)
                if t + 3 < NT:
                    tf = t + 3
                    e.wait_ge(vVr1, tf + 1)
                    ln_apply(e, ssync, vr1, tf % 4, negm1, r01, nb1, sR01, tf + 1)
                    e.wait_ge(tU0T, t + 2)
                    e.activation(u0b[:, tf % 2, :], xt[:, tf % 4, :], AF.Identity,
                                 scale=r01[:, :], bias=nb1[:, :]).then_inc(sU0, 1)
                e.wait_ge(tV, t + 1)
                e.copy(vb[:, :, 0:64], pA2_heads).then_inc(sVb, 1)
                if t % NJ == NJ - 1:
                    s = t // NJ
                    e.wait_ge(tKvz, t + 1)
                    for bi, pK in [(0, pKa), (1, pKb)]:
                        e.copy(kvz_sb[0:64, s, 4*bi:4*bi+4, 0:64], pK[0:64, :, 0:64])
                        e.copy(kvz_sb[64:128, s, 4*bi:4*bi+4, 64:128], pK[64:128, :, 0:64])
                        e.copy(zexp[0:64, s, 4*bi:4*bi+4, 0:1], pK[0:64, :, 64:65])
                        mm = e.copy(zexp[64:128, s, 4*bi:4*bi+4, 1:2], pK[64:128, :, 64:65])
                    mm.then_inc(sKvzE, 1)
            for t in range(1, NT + 2):
                if t <= NT:
                    e.wait_ge(tAttnT, t)
                    e.copy(attnfm[:, (t - 1) % 2, 0:512],
                           tpA[:, 0:4, :].rearrange("p a b -> p (a b)")).then_inc(sAttnfm, 1)
                    e.copy(attnfm[:, (t - 1) % 2, 512:1024],
                           tpA[:, 4:8, :].rearrange("p a b -> p (a b)")).then_inc(sAttnfm, 1)
                if 2 <= t:
                    e.wait_ge(tU2T, t - 1)
                    e.copy(u2fm[:, (t - 2) % 2, 0:512],
                           tpB[:, 0:4, :].rearrange("p a b -> p (a b)")).then_inc(sU2fm, 1)
                    e.copy(u2fm[:, (t - 2) % 2, 512:1024],
                           tpB[:, 4:8, :].rearrange("p a b -> p (a b)")).then_inc(sU2fm, 1)
                if t <= NT:
                    tw = t - 1
                    e.wait_ge(vY2, t)
                    e.activation(u1b[:, 0, :], y2t[:, tw % 2, :], AF.Square,
                                 accum_out=sq3b[:, :]).then_inc(sSq3, 1)
                    e.wait_ge(vVr3, t)
                    ln_apply(e, ssync, vr3, tw % 2, negm3, r03, nb3, sR03, t)
                    if t >= 3:
                        e.wait_ge(tU2T, t - 2)
                    e.activation(u2b[:, tw % 2, :], y2t[:, tw % 2, :], AF.Identity,
                                 scale=r03[:, :], bias=nb3[:, :]).then_inc(sU2, 1)

        def transposes8(e, src, tp, incsem):
            for c in range(8):
                mm = e.transpose(tp[:, c, :], src[:, c*P:(c+1)*P], ident[:, :])
            mm.then_inc(incsem, 1)

        def proj(e, lhs_fm, w_sb, dst, bias_row, incsem, cw=None, hw=None):
            last = (bias_row is None)
            if hw is None:
                for c in range(8):
                    if cw is not None and c % 4 == 0:
                        e.wait_ge(cw[0], cw[1] + c // 4)
                    for h in range(2):
                        mm = e.matmul(dst[:, h*512:(h+1)*512], lhs_fm[:, c*P:(c+1)*P],
                                      w_sb[:, c, h*512:(h+1)*512], start=(c == 0),
                                      stop=(last and c == 7))
                if bias_row is not None:
                    for h in range(2):
                        mm = e.matmul(dst[:, h*512:(h+1)*512], ones_row[0:1, :],
                                      bias_row[0:1, h*512:(h+1)*512], start=False, stop=True)
            else:
                for h in range(2):
                    e.wait_ge(hw[0], hw[1] + h)
                    for c in range(8):
                        mm = e.matmul(dst[:, h*512:(h+1)*512], lhs_fm[:, c*P:(c+1)*P],
                                      w_sb[:, c, h*512:(h+1)*512], start=(c == 0),
                                      stop=(last and c == 7))
                    if bias_row is not None:
                        mm = e.matmul(dst[:, h*512:(h+1)*512], ones_row[0:1, :],
                                      bias_row[0:1, h*512:(h+1)*512], start=False, stop=True)
            mm.then_inc(incsem, 1)

        @block.tensor
        def _(e):
            e.wait_ge(vIdent, 1)
            e.wait_ge(sSw, 8)
            e.wait_ge(gOnes, 1)
            e.wait_ge(gBias, 16 * 4)
            e.wait_ge(gVb, 2)
            e.wait_ge(gKz, 2)
            e.wait_ge(sU0, 1)
            transposes8(e, u0b[:, 0, :], tpB, tU0T)
            e.wait_ge(sU0, 2)
            e.wait_ge(sU0fm, 2)
            transposes8(e, u0b[:, 1, :], tpB, tU0T)
            proj(e, u0fm[:, 0, :], sw, pA1[:, :], None, tFfn1)
            e.wait_ge(sU1, 1)
            transposes8(e, u1b[:, 0, :], tpA, tU1T)
            for t in range(NT):
                s, j = t // NJ, t % NJ
                if t + 1 < NT:
                    e.wait_ge(vY1, t + 1)
                    proj(e, u0fm[:, (t + 1) % 2, :], sw, pA1[:, :], None, tFfn1,
                         cw=(sU0fm, 2 * t + 3))
                if t >= 1:
                    tl = t - 1
                    sl, jl = tl // NJ, tl % NJ
                    e.wait_ge(vPhiK, tl + 1)
                    e.wait_ge(sVb, tl + 1)
                    if jl == 0:
                        if sl >= 1:
                            e.wait_ge(sKvzE, sl)
                        for pK in (pKa, pKb):
                            for c4 in range(4):
                                e.matmul(pK[:, c4, 0:66], ones_row[0:1, :],
                                         ident[0:1, 1:67], start=(c4 == 0),
                                         stop=False, skip_group_check=True)
                    for h in range(NH):
                        pK = pKa if h < 8 else pKb
                        c4 = (h // 2) % 4
                        r = HD * (h % 2)
                        mm = e.matmul(pK[r:r+HD, c4, 0:66], phiK[:, h*HD:(h+1)*HD],
                                      vb[:, h, 0:66],
                                      start=False,
                                      stop=(jl == NJ - 1 and h in (7, NH - 1)),
                                      tile_position=(0, r), skip_group_check=True)
                    mm.then_inc(tKvz, 1)
                if t == 0:
                    e.wait_ge(gWq, 16)
                if t >= 1:
                    e.wait_ge(sVb, t)
                proj(e, u1fm[:, t % 2, :], wq, pA2[:, :], bqr, tQ,
                     cw=(sU1fm, 2 * t + 1))
                if t + 2 < NT:
                    e.wait_ge(sU0, t + 3)
                    e.wait_ge(sU0fm, 2 * t + 4)
                    transposes8(e, u0b[:, t % 2, :], tpB, tU0T)
                if t == 0:
                    e.wait_ge(gWk, 16)
                proj(e, u1fm[:, t % 2, :], wk, pA2[:, :], bkr, tK,
                     hw=(vKbQ, 2 * t + 1))
                if t + 1 < NT:
                    e.wait_ge(sU1, t + 2)
                    e.wait_ge(sU1fm, 2 * t + 2)
                    transposes8(e, u1b[:, (t + 1) % 2, :], tpA, tU1T)
                if t == 0:
                    e.wait_ge(gWv, 16)
                proj(e, u1fm[:, t % 2, :], wv, pA2[:, :], bvr, tV,
                     hw=(vKbK, 2 * t + 1))
            e.wait_ge(vPhiK, NT)
            e.wait_ge(sVb, NT)
            for h in range(NH):
                pK = pKa if h < 8 else pKb
                c4 = (h // 2) % 4
                r = HD * (h % 2)
                mm = e.matmul(pK[r:r+HD, c4, 0:66], phiK[:, h*HD:(h+1)*HD], vb[:, h, 0:66],
                              start=False, stop=(h in (7, NH - 1)), tile_position=(0, r),
                              skip_group_check=True)
            mm.then_inc(tKvz, 1)
            e.wait_ge(gWo, 16)
            e.wait_ge(sKvzE, SEQ_PER_CORE)
            e.wait_ge(vKbQ, 2 * NT)
            e.wait_ge(sVb, NT)
            for t in range(NT + 3):
                if 3 <= t <= NT + 2:
                    e.wait_ge(vY2, min(t - 1, NT))
                    if t > NT:
                        e.wait_ge(vY3, t - 3)
                    proj(e, u2fm[:, (t - 3) % 2, :], sw, pA2[:, :], None, tFfn2,
                         cw=(sU2fm, 2 * t - 5))
                if t < NT:
                    s = t // NJ
                    p2 = t % 2
                    e.wait_ge(dQi[p2], 16 * (t // 2 + 1))
                    if t >= 1:
                        e.wait_ge(vAttn, t)
                    for c in range(8):
                        e.matmul(pA1_pairs[:, c, :], qfin[:, p2, c*P:(c+1)*P], kvz_sb[:, s, c, :],
                                 start=True, stop=True)
                    for c in range(8):
                        mm = e.matmul(pden[:, 2*c:2*c+2], qfin[:, p2, c*P:(c+1)*P], zexp[:, s, c, :],
                                      start=True, stop=True)
                    mm.then_inc(tNum, 1)
                if 2 <= t <= NT + 1:
                    e.wait_ge(sU2, t - 1)
                    if t >= 3:
                        e.wait_ge(sU2fm, 2 * t - 4)
                    transposes8(e, u2b[:, (t - 2) % 2, :], tpB, tU2T)
                if 1 <= t <= NT:
                    if t >= 3:
                        e.wait_ge(vY3, t - 2)
                    proj(e, attnfm[:, (t - 1) % 2, :], wo, pA2[:, :], bor, tWo,
                         cw=(sAttnfm, 2 * t - 1))
                if t < NT:
                    e.wait_ge(vAttn, t + 1)
                    if t >= 1:
                        e.wait_ge(sAttnfm, 2 * t)
                    transposes8(e, attn[:, t % 2, :], tpA, tAttnT)

    return nc


def kernel(L0_attn_x, L0_attn_Wq, L0_attn_bq, L0_attn_Wk, L0_attn_bk,
           L0_attn_Wv, L0_attn_bv, L0_attn_Wo, L0_attn_bo, L0_ffn_weight):
    xs = np.ascontiguousarray(np.asarray(L0_attn_x, dtype=np.float32))
    wqT = np.ascontiguousarray(np.asarray(L0_attn_Wq, dtype=np.float32).T)
    wkT = np.ascontiguousarray(np.asarray(L0_attn_Wk, dtype=np.float32).T)
    wvT = np.ascontiguousarray(np.asarray(L0_attn_Wv, dtype=np.float32).T)
    woT = np.ascontiguousarray(np.asarray(L0_attn_Wo, dtype=np.float32).T)
    wfT = np.ascontiguousarray(np.asarray(L0_ffn_weight, dtype=np.float32).T)
    biases = {
        "bq": np.ascontiguousarray(np.asarray(L0_attn_bq, dtype=np.float32).reshape(1, D)),
        "bk": np.ascontiguousarray(np.asarray(L0_attn_bk, dtype=np.float32).reshape(1, D)),
        "bv": np.ascontiguousarray(np.asarray(L0_attn_bv, dtype=np.float32).reshape(1, D)),
        "bo": np.ascontiguousarray(np.asarray(L0_attn_bo, dtype=np.float32).reshape(1, D)),
    }
    nc = build_nc()
    in_maps = []
    for k in range(NCORE):
        m = {"x": np.ascontiguousarray(xs[k*SEQ_PER_CORE:(k+1)*SEQ_PER_CORE]),
             "wqT": wqT, "wkT": wkT, "wvT": wvT, "woT": woT, "wfT": wfT}
        m.update(biases)
        in_maps.append(m)
    res = run_bass_kernel_spmd(nc, in_maps, core_ids=list(range(NCORE)))
    return np.concatenate([np.asarray(res.results[k]["out"]) for k in range(NCORE)], axis=0)


# version 28
# speedup vs baseline: 1.0714x; 1.0416x over previous
import numpy as np
from contextlib import ExitStack

import concourse.bass as bass
import concourse.mybir as mybir
from concourse.bass_utils import run_bass_kernel_spmd

F32 = mybir.dt.float32
BF16 = mybir.dt.bfloat16
AF = mybir.ActivationFunctionType
ALU = mybir.AluOpType
AX = mybir.AxisListType

NCORE = 8
SEQ_PER_CORE = 4
SEQ_LEN = 2048
D = 1024
P = 128
NJ = SEQ_LEN // P
NT = SEQ_PER_CORE * NJ
NH = 16
HD = 64
LN_EPS = 1e-5
ATT_EPS = 1e-6
INV_D = 1.0 / D
SET = 8 * 16


def build_nc():
    nc = bass.Bass("TRN2", target_bir_lowering=False)

    x = nc.dram_tensor("x", [SEQ_PER_CORE, SEQ_LEN, D], F32, kind="ExternalInput").ap()
    wqT = nc.dram_tensor("wqT", [D, D], F32, kind="ExternalInput").ap()
    wkT = nc.dram_tensor("wkT", [D, D], F32, kind="ExternalInput").ap()
    wvT = nc.dram_tensor("wvT", [D, D], F32, kind="ExternalInput").ap()
    woT = nc.dram_tensor("woT", [D, D], F32, kind="ExternalInput").ap()
    wfT = nc.dram_tensor("wfT", [D, D], F32, kind="ExternalInput").ap()
    bq = nc.dram_tensor("bq", [1, D], F32, kind="ExternalInput").ap()
    bk = nc.dram_tensor("bk", [1, D], F32, kind="ExternalInput").ap()
    bv = nc.dram_tensor("bv", [1, D], F32, kind="ExternalInput").ap()
    bo = nc.dram_tensor("bo", [1, D], F32, kind="ExternalInput").ap()
    out = nc.dram_tensor("out", [SEQ_PER_CORE, SEQ_LEN, D], F32, kind="ExternalOutput").ap()
    y1scr = nc.dram_tensor("y1scr", [NT, P, D], F32, kind="Internal").ap()
    qscr = nc.dram_tensor("qscr", [NT, P, D], BF16, kind="Internal").ap()

    es = ExitStack()
    with es:
        sb = lambda name, shape, dt: es.enter_context(nc.sbuf_tensor(name, shape, dt))
        ps = lambda name, shape, dt: es.enter_context(nc.psum_tensor(name, shape, dt))
        sem = lambda name: es.enter_context(nc.semaphore(name))

        wq = sb("wq", [P, 8, D], BF16)
        wk = sb("wk", [P, 8, D], BF16)
        wv = sb("wv", [P, 8, D], BF16)
        wo = sb("wo", [P, 8, D], BF16)
        sw = sb("sw", [P, 8, D], BF16)
        stg = sb("stg", [P, 1, D], F32)
        bqr = sb("bqr", [1, D], BF16)
        bkr = sb("bkr", [1, D], BF16)
        bvr = sb("bvr", [1, D], BF16)
        bor = sb("bor", [1, D], BF16)
        ones_row = sb("ones_row", [1, P], BF16)
        ident = sb("ident", [P, P], BF16)
        kvz_sb = sb("kvz_sb", [P, SEQ_PER_CORE, 8, P], BF16)
        zexp = sb("zexp", [P, SEQ_PER_CORE, 8, 2], BF16)

        xt = sb("xt", [P, 4, D], F32)
        u0b = sb("u0b", [P, 2, D], BF16)
        u0fm = sb("u0fm", [P, 2, D], BF16)
        y1t = sb("y1t", [P, 2, D], F32)
        u1b = sb("u1b", [P, 2, D], BF16)
        u1fm = sb("u1fm", [P, 2, D], BF16)
        kbQ = sb("kbQ", [P, D], BF16)
        kbK = sb("kbK", [P, D], BF16)
        mnb = sb("mnb", [P, D], BF16)
        mxb = sb("mxb", [P, D], BF16)
        eb = sb("eb", [P, D], BF16)
        phiQ = sb("phiQ", [P, 2, D], BF16)
        phiK = sb("phiK", [P, D], BF16)
        qfm = sb("qfm", [P, 2, D], BF16)
        vb = sb("vb", [P, NH, 66], BF16)

        y1in = sb("y1in", [P, 2, D], F32)
        qfin = sb("qfin", [P, 2, D], BF16)
        attn = sb("attn", [P, 2, D], BF16)
        attnfm = sb("attnfm", [P, 2, D], BF16)
        y2t = sb("y2t", [P, 2, D], F32)
        u2b = sb("u2b", [P, 2, D], BF16)
        u2fm = sb("u2fm", [P, 2, D], BF16)
        y3t = sb("y3t", [P, 2, D], F32)
        ddb = sb("ddb", [P, NH], F32)
        rdb = sb("rdb", [P, NH], F32)

        red1b = sb("red1b", [P, 1], F32)
        sq1b = sb("sq1b", [P, 1], F32)
        red2b = sb("red2b", [P, 1], F32)
        sq2b = sb("sq2b", [P, 1], F32)
        red3b = sb("red3b", [P, 1], F32)
        sq3b = sb("sq3b", [P, 1], F32)
        mS1 = sb("mS1", [P, 3, 1], F32)
        mS2 = sb("mS2", [P, 3, 1], F32)
        mS3 = sb("mS3", [P, 3, 1], F32)
        negm1 = sb("negm1", [P, 4, 1], F32)
        negm2 = sb("negm2", [P, 2, 1], F32)
        negm3 = sb("negm3", [P, 2, 1], F32)
        lnb = sb("lnb", [P, 1], F32)
        vr1 = sb("vr1", [P, 4, 1], F32)
        vr2 = sb("vr2", [P, 2, 1], F32)
        vr3 = sb("vr3", [P, 2, 1], F32)
        r01 = sb("r01", [P, 1], F32)
        r02 = sb("r02", [P, 1], F32)
        r03 = sb("r03", [P, 1], F32)
        nb1 = sb("nb1", [P, 1], F32)
        nb2 = sb("nb2", [P, 1], F32)
        nb3 = sb("nb3", [P, 1], F32)

        tpA = ps("tpA", [P, 8, P], BF16)
        tpB = ps("tpB", [P, 8, P], BF16)
        pA1 = ps("pA1", [P, D], F32)
        pA2 = ps("pA2", [P, D], F32)
        pKa = ps("pKa", [P, 4, 66], F32)
        pKb = ps("pKb", [P, 4, 66], F32)

        pA1_pairs = pA1[:, :].rearrange("p (c n) -> p c n", n=P)
        pA1_heads = pA1[:, :].rearrange("p (g e) -> p g e", e=HD)
        pA2_heads = pA2[:, :].rearrange("p (g e) -> p g e", e=HD)
        pden = pKa[:, 0, 0:NH]

        gIota, gOnes, gVb, gKz, gBias = sem("gIota"), sem("gOnes"), sem("gVb"), sem("gKz"), sem("gBias")
        gWq, gWk, gWv, gWo = sem("gWq"), sem("gWk"), sem("gWv"), sem("gWo")
        gStg = sem("gStg")
        sSw = sem("sSw")
        dIn = [sem(f"dIn{i}") for i in range(4)]
        dY1o = [sem("dY1oA"), sem("dY1oB")]
        dQo = [sem("dQoA"), sem("dQoB")]
        dY1i = [sem("dY1iA"), sem("dY1iB")]
        dQi = [sem("dQiA"), sem("dQiB")]
        dY3o = [sem("dY3oA"), sem("dY3oB")]
        dQT = sem("dQT")
        vIdent = sem("vIdent")
        vVr1, vY1, vVr2 = sem("vVr1"), sem("vY1"), sem("vVr2")
        vKbQ, vPhiQ = sem("vKbQ"), sem("vPhiQ")
        vKbK, vPhiK = sem("vKbK"), sem("vPhiK")
        vAttn, vY2, vVr3, vY3 = sem("vAttn"), sem("vY2"), sem("vVr3"), sem("vY3")
        sR01, sU0, sU0fm, sR02, sU1, sU1fm = sem("sR01"), sem("sU0"), sem("sU0fm"), sem("sR02"), sem("sU1"), sem("sU1fm")
        sEQ, sEK, sVb, sKvzE = sem("sEQ"), sem("sEK"), sem("sVb"), sem("sKvzE")
        sAttnfm, sR03, sU2, sU2fm = sem("sAttnfm"), sem("sR03"), sem("sU2"), sem("sU2fm")
        sSq2, sSq3 = sem("sSq2"), sem("sSq3")
        vC = sem("vC")
        sC = sem("sC")
        tFfn1, tU0T, tU1T, tQ, tK, tV, tKvz = (sem("tFfn1"), sem("tU0T"), sem("tU1T"),
                                               sem("tQ"), sem("tK"), sem("tV"), sem("tKvz"))
        tNum, tAttnT, tWo, tU2T, tFfn2 = sem("tNum"), sem("tAttnT"), sem("tWo"), sem("tU2T"), sem("tFfn2")

        block = es.enter_context(nc.Block())

        @block.sync
        def _(e):
            for t0 in range(4):
                s, j = t0 // NJ, t0 % NJ
                e.dma_start(out=xt[:, t0 % 4, :], in_=x[s, j*P:(j+1)*P, :]).then_inc(dIn[t0 % 4], 16)
            for c in range(8):
                buf = stg[:, 0, :] if c % 2 == 0 else y3t[:, 0, :]
                if c >= 2:
                    e.wait_ge(sSw, c - 1)
                if c >= 1:
                    e.wait_ge(gStg, 16 * c)
                e.dma_start(out=buf, in_=wfT[c*P:(c+1)*P, :]).then_inc(gStg, 16)
            for t in range(NT):
                if t + 4 < NT:
                    tl = t + 4
                    s, j = tl // NJ, tl % NJ
                    e.wait_ge(sU0, t + 1)
                    e.wait_ge(vY1, t + 1)
                    e.wait_ge(dIn[tl % 4], 16 * (tl // 4))
                    e.dma_start(out=xt[:, tl % 4, :], in_=x[s, j*P:(j+1)*P, :]).then_inc(dIn[tl % 4], 16)
                e.wait_ge(vPhiQ, t + 1)
                if t >= 1:
                    e.wait_ge(dQT, SET * t)
                if t >= 2:
                    e.wait_ge(dQo[t % 2], 16 * (t // 2))
                for c in range(8):
                    e.dma_start(out=qfm[:, t % 2, c*P:(c+1)*P], in_=phiQ[:, t % 2, c*P:(c+1)*P],
                                transpose=True).then_inc(dQT, 16)
            for t in range(NT):
                s, j = t // NJ, t % NJ
                e.wait_ge(vY3, t + 1)
                if t >= 2:
                    e.wait_ge(dY3o[t % 2], 16 * (t // 2))
                e.dma_start(out=out[s, j*P:(j+1)*P, :], in_=y3t[:, t % 2, :]).then_inc(dY3o[t % 2], 16)
            e.wait_ge(dY3o[0], 16 * (NT // 2))
            e.wait_ge(dY3o[1], 16 * (NT // 2))

        @block.gpsimd
        def _(e):
            e.iota(attn[:, 0, 0:P], pattern=[[1, P]], channel_multiplier=-1,
                   allow_small_or_imprecise_dtypes=True).then_inc(gIota, 1)
            e.memset(ones_row[:, :], 1.0).then_inc(gOnes, 1)
            e.memset(vb[:, :, 64:65], 1.0).then_inc(gVb, 1)
            e.memset(vb[:, :, 65:66], 0.0).then_inc(gVb, 1)
            e.memset(kvz_sb[:, :, :, :], 0.0).then_inc(gKz, 1)
            e.memset(zexp[:, :, :, :], 0.0).then_inc(gKz, 1)
            for i, (bias_dst, bias_src) in enumerate([(bqr, bq), (bkr, bk), (bvr, bv), (bor, bo)]):
                if i >= 1:
                    e.wait_ge(gBias, 16 * i)
                e.dma_start(out=bias_dst[:, :], in_=bias_src).then_inc(gBias, 16)
            e.dma_start(out=wq[:, :, :], in_=wqT.rearrange("(c p) d -> p c d", p=P)).then_inc(gWq, 16)
            e.dma_start(out=wk[:, :, :], in_=wkT.rearrange("(c p) d -> p c d", p=P)).then_inc(gWk, 16)
            e.dma_start(out=wv[:, :, :], in_=wvT.rearrange("(c p) d -> p c d", p=P)).then_inc(gWv, 16)
            e.dma_start(out=wo[:, :, :], in_=woT.rearrange("(c p) d -> p c d", p=P)).then_inc(gWo, 16)
            for t in range(NT):
                e.wait_ge(vY1, t + 1)
                if t >= 2:
                    e.wait_ge(dY1o[t % 2], 16 * (t // 2))
                e.dma_start(out=y1scr[t], in_=y1t[:, t % 2, :]).then_inc(dY1o[t % 2], 16)
                e.wait_ge(dQT, SET * (t + 1))
                if t >= 2:
                    e.wait_ge(dQo[t % 2], 16 * (t // 2))
                e.dma_start(out=qscr[t], in_=qfm[:, t % 2, :]).then_inc(dQo[t % 2], 16)
                if t == min(34, NT - 1):
                    for tb in (0, 1):
                        e.dma_start(out=qfin[:, tb, :], in_=qscr[tb]).then_inc(dQi[tb], 16)
                        e.dma_start(out=y1in[:, tb, :], in_=y1scr[tb]).then_inc(dY1i[tb], 16)
            for s_ in (dY1o[0], dY1o[1], dQo[0], dQo[1]):
                e.wait_ge(s_, 16 * (NT // 2))
            for t in range(2, NT):
                e.wait_ge(tNum, t - 1)
                e.wait_ge(dQi[t % 2], 16 * (t // 2))
                e.dma_start(out=qfin[:, t % 2, :], in_=qscr[t]).then_inc(dQi[t % 2], 16)
                e.wait_ge(vY2, t - 1)
                e.wait_ge(dY1i[t % 2], 16 * (t // 2))
                e.dma_start(out=y1in[:, t % 2, :], in_=y1scr[t]).then_inc(dY1i[t % 2], 16)

        def ln_stats_multi(v, vsync, specs):
            for (redb, sqb, vrX, pi, negm, mS) in specs:
                ins = v.tensor_scalar(mS[:, 0, :], redb[:, :], INV_D, None, ALU.mult)
            vsync(v, ins)
            for (redb, sqb, vrX, pi, negm, mS) in specs:
                v.tensor_scalar(mS[:, 1, :], sqb[:, :], INV_D, LN_EPS, ALU.mult, ALU.add)
                v.tensor_tensor(mS[:, 2, :], mS[:, 0, :], mS[:, 0, :], ALU.mult)
                ins = v.tensor_scalar(negm[:, pi, :], mS[:, 0, :], -1.0, None, ALU.mult)
            vsync(v, ins)
            out = []
            for (redb, sqb, vrX, pi, negm, mS) in specs:
                out.append(v.tensor_tensor(vrX[:, pi, :], mS[:, 1, :], mS[:, 2, :], ALU.subtract))
            return out

        vc_n = [0]
        def vsync(v, ins):
            vc_n[0] += 1
            ins.then_inc(vC, 1)
            v.wait_ge(vC, vc_n[0])

        @block.vector
        def _(v):
            v.wait_ge(gIota, 1)
            v.tensor_single_scalar(ident[:, :], attn[:, 0, 0:P], 0.0, ALU.is_equal).then_inc(vIdent, 1)
            for t0 in (0, 1, 2):
                v.wait_ge(dIn[t0], 16)
                v.reduce_sum(red1b[:, :], xt[:, t0, :], axis=AX.X)
                vsync(v, v.scalar_tensor_tensor(y2t[:, 0, :], xt[:, t0, :], 0.0, xt[:, t0, :],
                                       ALU.bypass, ALU.mult, accum_out=sq1b[:, :]))
                ln_stats_multi(v, vsync, [(red1b, sq1b, vr1, t0, negm1, mS1)])[0].then_inc(vVr1, 1)
            for t0 in (0, 1):
                v.wait_ge(tU0T, t0 + 1)
                v.tensor_copy(u0fm[:, t0 % 2, 0:512], tpB[:, 0:4, :].rearrange("p a b -> p (a b)")).then_inc(sU0fm, 1)
                v.tensor_copy(u0fm[:, t0 % 2, 512:1024], tpB[:, 4:8, :].rearrange("p a b -> p (a b)")).then_inc(sU0fm, 1)
            v.wait_ge(tFfn1, 1)
            v.scalar_tensor_tensor(y1t[:, 0, :], pA1[:, :], 0.5, xt[:, 0, :],
                                   ALU.mult, ALU.add, accum_out=red2b[:, :]).then_inc(vY1, 1)
            v.wait_ge(sSq2, 1)
            ln_stats_multi(v, vsync, [(red2b, sq2b, vr2, 0, negm2, mS2)])[0].then_inc(vVr2, 1)
            for t in range(NT):
                p2 = t % 2
                v.wait_ge(tU1T, t + 1)
                v.tensor_copy(u1fm[:, p2, 0:512], tpA[:, 0:4, :].rearrange("p a b -> p (a b)")).then_inc(sU1fm, 1)
                v.tensor_copy(u1fm[:, p2, 512:1024], tpA[:, 4:8, :].rearrange("p a b -> p (a b)")).then_inc(sU1fm, 1)
                if t + 1 < NT:
                    v.wait_ge(tFfn1, t + 2)
                    v.wait_ge(sU1, t + 1)
                    if t >= 1:
                        v.wait_ge(dY1o[(t + 1) % 2], 16 * ((t + 1) // 2))
                    v.scalar_tensor_tensor(y1t[:, (t + 1) % 2, :], pA1[:, :], 0.5, xt[:, (t + 1) % 4, :],
                                           ALU.mult, ALU.add, accum_out=red2b[:, :]).then_inc(vY1, 1)
                if t + 3 < NT:
                    tf = t + 3
                    v.wait_ge(dIn[tf % 4], 16 * (tf // 4 + 1))
                    v.reduce_sum(red1b[:, :], xt[:, tf % 4, :], axis=AX.X)
                    vsync(v, v.scalar_tensor_tensor(y2t[:, 0, :], xt[:, tf % 4, :], 0.0, xt[:, tf % 4, :],
                                           ALU.bypass, ALU.mult, accum_out=sq1b[:, :]))
                specs = []
                if t + 1 < NT:
                    v.wait_ge(sSq2, t + 2)
                    if t >= 2:
                        v.wait_ge(sR02, t)
                    specs.append((red2b, sq2b, vr2, (t + 1) % 2, negm2, mS2))
                if t + 3 < NT:
                    if t >= 1:
                        v.wait_ge(sR01, t)
                    specs.append((red1b, sq1b, vr1, (t + 3) % 4, negm1, mS1))
                if specs:
                    outs = ln_stats_multi(v, vsync, specs)
                    if t + 1 < NT:
                        outs[0].then_inc(vVr2, 1)
                    if t + 3 < NT:
                        outs[-1].then_inc(vVr1, 1)
                if t + 2 < NT:
                    v.wait_ge(tU0T, t + 3)
                    v.tensor_copy(u0fm[:, t % 2, 0:512], tpB[:, 0:4, :].rearrange("p a b -> p (a b)")).then_inc(sU0fm, 1)
                    v.tensor_copy(u0fm[:, t % 2, 512:1024], tpB[:, 4:8, :].rearrange("p a b -> p (a b)")).then_inc(sU0fm, 1)
                v.wait_ge(tK, t + 1)
                v.tensor_copy(kbK[:, 0:512], pA2[:, 0:512]).then_inc(vKbK, 1)
                v.tensor_copy(kbK[:, 512:1024], pA2[:, 512:1024]).then_inc(vKbK, 1)
                if t >= 2:
                    v.wait_ge(dQT, SET * (t - 1))
                v.wait_ge(sEQ, t + 1)
                v.tensor_tensor(phiQ[:, p2, :], eb[:, :], mxb[:, :], ALU.add).then_inc(vPhiQ, 1)
                v.wait_ge(sEK, t + 1)
                v.tensor_tensor(phiK[:, :], eb[:, :], mxb[:, :], ALU.add).then_inc(vPhiK, 1)
            for t in range(NT + 3):
                if 3 <= t <= NT + 2:
                    tf = t - 3
                    v.wait_ge(tFfn2, t - 2)
                    if tf >= 2:
                        v.wait_ge(dY3o[tf % 2], 16 * (tf // 2))
                    v.scalar_tensor_tensor(y3t[:, tf % 2, :], pA2[:, :], 0.5, y2t[:, tf % 2, :],
                                           ALU.mult, ALU.add).then_inc(vY3, 1)
                if t < NT:
                    v.wait_ge(tNum, t + 1)
                    vsync(v, v.tensor_scalar(ddb[:, :], pden, ATT_EPS, None, ALU.add))
                    vsync(v, v.reciprocal(rdb[:, :], ddb[:, :]))
                    if t >= 2:
                        v.wait_ge(tAttnT, t - 1)
                    v.tensor_tensor(attn[:, t % 2, :].rearrange("p (g e) -> p g e", e=HD),
                                    pA1_heads,
                                    rdb[:, :].unsqueeze(2).broadcast_to([P, NH, HD]),
                                    ALU.mult).then_inc(vAttn, 1)
                if 1 <= t <= NT:
                    tw = t - 1
                    if 3 <= t <= NT + 2:
                        v.wait_ge(vY3, t - 2)
                    v.wait_ge(tWo, t)
                    v.wait_ge(dY1i[tw % 2], 16 * (tw // 2 + 1))
                    if t >= 3:
                        v.wait_ge(sU2, t - 2)
                    v.scalar_tensor_tensor(y2t[:, tw % 2, :], pA2[:, :], 0.0, y1in[:, tw % 2, :],
                                           ALU.bypass, ALU.add, accum_out=red3b[:, :]).then_inc(vY2, 1)
                    v.wait_ge(sSq3, t)
                    if t >= 3:
                        v.wait_ge(sR03, t - 2)
                    ln_stats_multi(v, vsync, [(red3b, sq3b, vr3, tw % 2, negm3, mS3)])[0].then_inc(vVr3, 1)

        sc_n = [0]
        def ssync(e, ins):
            sc_n[0] += 1
            ins.then_inc(sC, 1)
            e.wait_ge(sC, sc_n[0])

        def ln_apply(e, ssync, vrX, pi, negm, r0, nb, incR, nR):
            ssync(e, e.activation(lnb[:, :], vrX[:, pi, :], AF.Ln))
            ssync(e, e.activation(r0[:, :], lnb[:, :], AF.Exp, scale=-0.5))
            e.activation(nb[:, :], negm[:, pi, :], AF.Identity, scale=r0[:, :]).then_inc(incR, 1)
            e.wait_ge(incR, nR)

        @block.scalar
        def _(e):
            for c in range(8):
                e.wait_ge(gStg, 16 * (c + 1))
                e.sign(sw[:, c, :], stg[:, 0, :] if c % 2 == 0 else y3t[:, 0, :]).then_inc(sSw, 1)
            for t0 in (0, 1, 2):
                e.wait_ge(vVr1, t0 + 1)
                ln_apply(e, ssync, vr1, t0, negm1, r01, nb1, sR01, t0 + 1)
                if t0 == 2:
                    e.wait_ge(tU0T, 1)
                e.activation(u0b[:, t0 % 2, :], xt[:, t0, :], AF.Identity,
                             scale=r01[:, :], bias=nb1[:, :]).then_inc(sU0, 1)
            e.wait_ge(vY1, 1)
            e.activation(u2b[:, 0, :], y1t[:, 0, :], AF.Square,
                         accum_out=sq2b[:, :]).then_inc(sSq2, 1)
            e.wait_ge(vVr2, 1)
            ln_apply(e, ssync, vr2, 0, negm2, r02, nb2, sR02, 1)
            e.activation(u1b[:, 0, :], y1t[:, 0, :], AF.Identity,
                         scale=r02[:, :], bias=nb2[:, :]).then_inc(sU1, 1)
            for t in range(NT):
                if t + 1 < NT:
                    e.wait_ge(vY1, t + 2)
                    e.activation(u2b[:, 0, :], y1t[:, (t + 1) % 2, :], AF.Square,
                                 accum_out=sq2b[:, :]).then_inc(sSq2, 1)
                e.wait_ge(tQ, t + 1)
                e.copy(kbQ[:, 0:512], pA2[:, 0:512]).then_inc(vKbQ, 1)
                e.copy(kbQ[:, 512:1024], pA2[:, 512:1024]).then_inc(vKbQ, 1)
                e.wait_ge(vKbQ, 2 * t + 2)
                if t + 1 < NT:
                    e.wait_ge(vVr2, t + 2)
                    ln_apply(e, ssync, vr2, (t + 1) % 2, negm2, r02, nb2, sR02, t + 2)
                    if t >= 1:
                        e.wait_ge(tU1T, t)
                    e.activation(u1b[:, (t + 1) % 2, :], y1t[:, (t + 1) % 2, :], AF.Identity,
                                 scale=r02[:, :], bias=nb2[:, :]).then_inc(sU1, 1)
                if t >= 1:
                    e.wait_ge(vPhiK, t)
                ssync(e, e.activation(mnb[:, :], kbQ[:, :], AF.Relu, scale=-1.0))
                e.activation(eb[:, :], mnb[:, :], AF.Exp, scale=-1.0)
                e.activation(mxb[:, :], kbQ[:, :], AF.Relu).then_inc(sEQ, 1)
                e.wait_ge(vKbK, 2 * t + 2)
                e.wait_ge(vPhiQ, t + 1)
                ssync(e, e.activation(mnb[:, :], kbK[:, :], AF.Relu, scale=-1.0))
                e.activation(eb[:, :], mnb[:, :], AF.Exp, scale=-1.0)
                e.activation(mxb[:, :], kbK[:, :], AF.Relu).then_inc(sEK, 1)
                if t + 3 < NT:
                    tf = t + 3
                    e.wait_ge(vVr1, tf + 1)
                    ln_apply(e, ssync, vr1, tf % 4, negm1, r01, nb1, sR01, tf + 1)
                    e.wait_ge(tU0T, t + 2)
                    e.activation(u0b[:, tf % 2, :], xt[:, tf % 4, :], AF.Identity,
                                 scale=r01[:, :], bias=nb1[:, :]).then_inc(sU0, 1)
                e.wait_ge(tV, t + 1)
                e.copy(vb[:, :, 0:64], pA2_heads).then_inc(sVb, 1)
                if t % NJ == NJ - 1:
                    s = t // NJ
                    e.wait_ge(tKvz, t + 1)
                    for bi, pK in [(0, pKa), (1, pKb)]:
                        e.copy(kvz_sb[0:64, s, 4*bi:4*bi+4, 0:64], pK[0:64, :, 0:64])
                        e.copy(kvz_sb[64:128, s, 4*bi:4*bi+4, 64:128], pK[64:128, :, 0:64])
                        e.copy(zexp[0:64, s, 4*bi:4*bi+4, 0:1], pK[0:64, :, 64:65])
                        mm = e.copy(zexp[64:128, s, 4*bi:4*bi+4, 1:2], pK[64:128, :, 64:65])
                    mm.then_inc(sKvzE, 1)
            for t in range(1, NT + 2):
                if t <= NT:
                    e.wait_ge(tAttnT, t)
                    e.copy(attnfm[:, (t - 1) % 2, 0:512],
                           tpA[:, 0:4, :].rearrange("p a b -> p (a b)")).then_inc(sAttnfm, 1)
                    e.copy(attnfm[:, (t - 1) % 2, 512:1024],
                           tpA[:, 4:8, :].rearrange("p a b -> p (a b)")).then_inc(sAttnfm, 1)
                if 2 <= t:
                    e.wait_ge(tU2T, t - 1)
                    e.copy(u2fm[:, (t - 2) % 2, 0:512],
                           tpB[:, 0:4, :].rearrange("p a b -> p (a b)")).then_inc(sU2fm, 1)
                    e.copy(u2fm[:, (t - 2) % 2, 512:1024],
                           tpB[:, 4:8, :].rearrange("p a b -> p (a b)")).then_inc(sU2fm, 1)
                if t <= NT:
                    tw = t - 1
                    e.wait_ge(vY2, t)
                    e.activation(u1b[:, 0, :], y2t[:, tw % 2, :], AF.Square,
                                 accum_out=sq3b[:, :]).then_inc(sSq3, 1)
                    e.wait_ge(vVr3, t)
                    ln_apply(e, ssync, vr3, tw % 2, negm3, r03, nb3, sR03, t)
                    if t >= 3:
                        e.wait_ge(tU2T, t - 2)
                    e.activation(u2b[:, tw % 2, :], y2t[:, tw % 2, :], AF.Identity,
                                 scale=r03[:, :], bias=nb3[:, :]).then_inc(sU2, 1)

        def transposes8(e, src, tp, incsem):
            for c in range(8):
                mm = e.transpose(tp[:, c, :], src[:, c*P:(c+1)*P], ident[:, :])
            mm.then_inc(incsem, 1)

        def proj(e, lhs_fm, w_sb, dst, bias_row, incsem, cw=None, hw=None):
            last = (bias_row is None)
            if hw is None:
                for c in range(8):
                    if cw is not None and c % 4 == 0:
                        e.wait_ge(cw[0], cw[1] + c // 4)
                    for h in range(2):
                        mm = e.matmul(dst[:, h*512:(h+1)*512], lhs_fm[:, c*P:(c+1)*P],
                                      w_sb[:, c, h*512:(h+1)*512], start=(c == 0),
                                      stop=(last and c == 7))
                if bias_row is not None:
                    for h in range(2):
                        mm = e.matmul(dst[:, h*512:(h+1)*512], ones_row[0:1, :],
                                      bias_row[0:1, h*512:(h+1)*512], start=False, stop=True)
            else:
                for h in range(2):
                    e.wait_ge(hw[0], hw[1] + h)
                    for c in range(8):
                        mm = e.matmul(dst[:, h*512:(h+1)*512], lhs_fm[:, c*P:(c+1)*P],
                                      w_sb[:, c, h*512:(h+1)*512], start=(c == 0),
                                      stop=(last and c == 7))
                    if bias_row is not None:
                        mm = e.matmul(dst[:, h*512:(h+1)*512], ones_row[0:1, :],
                                      bias_row[0:1, h*512:(h+1)*512], start=False, stop=True)
            mm.then_inc(incsem, 1)

        @block.tensor
        def _(e):
            e.wait_ge(vIdent, 1)
            e.wait_ge(sSw, 8)
            e.wait_ge(gOnes, 1)
            e.wait_ge(gBias, 16 * 4)
            e.wait_ge(gVb, 2)
            e.wait_ge(gKz, 2)
            e.wait_ge(sU0, 1)
            transposes8(e, u0b[:, 0, :], tpB, tU0T)
            e.wait_ge(sU0, 2)
            e.wait_ge(sU0fm, 2)
            transposes8(e, u0b[:, 1, :], tpB, tU0T)
            proj(e, u0fm[:, 0, :], sw, pA1[:, :], None, tFfn1)
            e.wait_ge(sU1, 1)
            transposes8(e, u1b[:, 0, :], tpA, tU1T)
            for t in range(NT):
                s, j = t // NJ, t % NJ
                if t + 1 < NT:
                    e.wait_ge(vY1, t + 1)
                    proj(e, u0fm[:, (t + 1) % 2, :], sw, pA1[:, :], None, tFfn1,
                         cw=(sU0fm, 2 * t + 3))
                if t >= 1:
                    tl = t - 1
                    sl, jl = tl // NJ, tl % NJ
                    e.wait_ge(vPhiK, tl + 1)
                    e.wait_ge(sVb, tl + 1)
                    if jl == 0:
                        if sl >= 1:
                            e.wait_ge(sKvzE, sl)
                        for pK in (pKa, pKb):
                            for c4 in range(4):
                                e.matmul(pK[:, c4, 0:66], ones_row[0:1, :],
                                         ident[0:1, 1:67], start=(c4 == 0),
                                         stop=False, skip_group_check=True)
                    for h in range(NH):
                        pK = pKa if h < 8 else pKb
                        c4 = (h // 2) % 4
                        r = HD * (h % 2)
                        mm = e.matmul(pK[r:r+HD, c4, 0:66], phiK[:, h*HD:(h+1)*HD],
                                      vb[:, h, 0:66],
                                      start=False,
                                      stop=(jl == NJ - 1 and h in (7, NH - 1)),
                                      tile_position=(0, r), skip_group_check=True)
                    mm.then_inc(tKvz, 1)
                if t == 0:
                    e.wait_ge(gWq, 16)
                if t >= 1:
                    e.wait_ge(sVb, t)
                proj(e, u1fm[:, t % 2, :], wq, pA2[:, :], bqr, tQ,
                     cw=(sU1fm, 2 * t + 1))
                if t + 2 < NT:
                    e.wait_ge(sU0, t + 3)
                    e.wait_ge(sU0fm, 2 * t + 4)
                    transposes8(e, u0b[:, t % 2, :], tpB, tU0T)
                if t == 0:
                    e.wait_ge(gWk, 16)
                proj(e, u1fm[:, t % 2, :], wk, pA2[:, :], bkr, tK,
                     hw=(vKbQ, 2 * t + 1))
                if t + 1 < NT:
                    e.wait_ge(sU1, t + 2)
                    e.wait_ge(sU1fm, 2 * t + 2)
                    transposes8(e, u1b[:, (t + 1) % 2, :], tpA, tU1T)
                if t == 0:
                    e.wait_ge(gWv, 16)
                proj(e, u1fm[:, t % 2, :], wv, pA2[:, :], bvr, tV,
                     hw=(vKbK, 2 * t + 1))
            e.wait_ge(vPhiK, NT)
            e.wait_ge(sVb, NT)
            for h in range(NH):
                pK = pKa if h < 8 else pKb
                c4 = (h // 2) % 4
                r = HD * (h % 2)
                mm = e.matmul(pK[r:r+HD, c4, 0:66], phiK[:, h*HD:(h+1)*HD], vb[:, h, 0:66],
                              start=False, stop=(h in (7, NH - 1)), tile_position=(0, r),
                              skip_group_check=True)
            mm.then_inc(tKvz, 1)
            e.wait_ge(gWo, 16)
            e.wait_ge(sKvzE, SEQ_PER_CORE)
            e.wait_ge(vKbQ, 2 * NT)
            e.wait_ge(sVb, NT)
            for t in range(NT + 3):
                if 3 <= t <= NT + 2:
                    e.wait_ge(vY2, min(t - 1, NT))
                    if t > NT:
                        e.wait_ge(vY3, t - 3)
                    proj(e, u2fm[:, (t - 3) % 2, :], sw, pA2[:, :], None, tFfn2,
                         cw=(sU2fm, 2 * t - 5))
                if t < NT:
                    s = t // NJ
                    p2 = t % 2
                    e.wait_ge(dQi[p2], 16 * (t // 2 + 1))
                    if t >= 1:
                        e.wait_ge(vAttn, t)
                    for c in range(8):
                        e.matmul(pA1_pairs[:, c, :], qfin[:, p2, c*P:(c+1)*P], kvz_sb[:, s, c, :],
                                 start=True, stop=True)
                    for c in range(8):
                        mm = e.matmul(pden[:, 2*c:2*c+2], qfin[:, p2, c*P:(c+1)*P], zexp[:, s, c, :],
                                      start=True, stop=True)
                    mm.then_inc(tNum, 1)
                if 2 <= t <= NT + 1:
                    e.wait_ge(sU2, t - 1)
                    if t >= 3:
                        e.wait_ge(sU2fm, 2 * t - 4)
                    transposes8(e, u2b[:, (t - 2) % 2, :], tpB, tU2T)
                if 1 <= t <= NT:
                    if t >= 3:
                        e.wait_ge(vY3, t - 2)
                    proj(e, attnfm[:, (t - 1) % 2, :], wo, pA2[:, :], bor, tWo,
                         cw=(sAttnfm, 2 * t - 1))
                if t < NT:
                    e.wait_ge(vAttn, t + 1)
                    if t >= 1:
                        e.wait_ge(sAttnfm, 2 * t)
                    transposes8(e, attn[:, t % 2, :], tpA, tAttnT)

    return nc


def kernel(L0_attn_x, L0_attn_Wq, L0_attn_bq, L0_attn_Wk, L0_attn_bk,
           L0_attn_Wv, L0_attn_bv, L0_attn_Wo, L0_attn_bo, L0_ffn_weight):
    xs = np.ascontiguousarray(np.asarray(L0_attn_x, dtype=np.float32))
    wqT = np.ascontiguousarray(np.asarray(L0_attn_Wq, dtype=np.float32).T)
    wkT = np.ascontiguousarray(np.asarray(L0_attn_Wk, dtype=np.float32).T)
    wvT = np.ascontiguousarray(np.asarray(L0_attn_Wv, dtype=np.float32).T)
    woT = np.ascontiguousarray(np.asarray(L0_attn_Wo, dtype=np.float32).T)
    wfT = np.ascontiguousarray(np.asarray(L0_ffn_weight, dtype=np.float32).T)
    biases = {
        "bq": np.ascontiguousarray(np.asarray(L0_attn_bq, dtype=np.float32).reshape(1, D)),
        "bk": np.ascontiguousarray(np.asarray(L0_attn_bk, dtype=np.float32).reshape(1, D)),
        "bv": np.ascontiguousarray(np.asarray(L0_attn_bv, dtype=np.float32).reshape(1, D)),
        "bo": np.ascontiguousarray(np.asarray(L0_attn_bo, dtype=np.float32).reshape(1, D)),
    }
    nc = build_nc()
    in_maps = []
    for k in range(NCORE):
        m = {"x": np.ascontiguousarray(xs[k*SEQ_PER_CORE:(k+1)*SEQ_PER_CORE]),
             "wqT": wqT, "wkT": wkT, "wvT": wvT, "woT": woT, "wfT": wfT}
        m.update(biases)
        in_maps.append(m)
    res = run_bass_kernel_spmd(nc, in_maps, core_ids=list(range(NCORE)))
    return np.concatenate([np.asarray(res.results[k]["out"]) for k in range(NCORE)], axis=0)
